# version 03
# baseline (speedup 1.0000x reference)
import numpy as np
import ml_dtypes
from contextlib import ExitStack

import concourse.bass as bass
import concourse.mybir as mybir
from concourse.bass_utils import run_bass_kernel_spmd

F32 = mybir.dt.float32
BF16 = mybir.dt.bfloat16
AF = mybir.ActivationFunctionType
ALU = mybir.AluOpType
AX = mybir.AxisListType.X

NCORES = 8
NSEQ = 2
S = 2048
T = NSEQ * S
D = 1024
DH = 4096
NH = 16
HD = 64
CS = 16
NE = 8
FE = 2048

BF = ml_dtypes.bfloat16


def np2dt(dt):
    if dt == np.float32:
        return F32
    if dt == BF:
        return BF16
    raise ValueError(dt)


class Rec:

    ENGINES = ("sync", "scalar", "vector", "gpsimd", "tensor")

    def __init__(self, dsem, csem):
        self.dsem = dsem
        self.csem = csem
        self.streams = {e: [] for e in self.ENGINES}
        self.d = 0
        self.c = 0
        self.lw_d = {e: 0 for e in self.ENGINES}
        self.lw_co = {e: 0 for e in self.ENGINES}
        self.oc = {e: 0 for e in self.ENGINES}

    SKIP_OWN = False

    def _wait_all(self, eng):
        st = self.streams[eng]
        if self.d > self.lw_d[eng]:
            st.append(("wait_ge", (self.dsem, self.d * 16), {}, None))
            self.lw_d[eng] = self.d
        c_other = (self.c - self.oc[eng]) if self.SKIP_OWN else self.c
        if c_other > self.lw_co[eng]:
            st.append(("wait_ge", (self.csem, self.c), {}, None))
            self.lw_co[eng] = c_other

    def dma(self, dst, src, eng="sync", **kw):
        self._wait_all(eng)
        self.streams[eng].append(("dma_start", (dst, src), kw, (self.dsem, 16)))
        self.d += 1

    def comp(self, eng, meth, *args, **kw):
        self._wait_all(eng)
        self.streams[eng].append((meth, args, kw, (self.csem, 1)))
        self.c += 1
        self.oc[eng] += 1

    def finish(self):
        self.streams["sync"].append(("wait_ge", (self.dsem, self.d * 16), {}, None))
        self.streams["sync"].append(("wait_ge", (self.csem, self.c), {}, None))

    def flush_into(self, block):
        for eng in self.ENGINES:
            stream = self.streams[eng]
            if not stream:
                continue

            def make(fn_stream):
                def _(engobj):
                    for meth, args, kw, inc in fn_stream:
                        ins = getattr(engobj, meth)(*args, **kw)
                        if inc is not None:
                            ins.then_inc(*inc)
                return _

            getattr(block, eng)(make(stream))
        self.streams = {e: [] for e in self.ENGINES}

    def mm(self, out, lhsT, rhs, start=True, stop=True):
        self.comp("tensor", "matmul", out, lhsT, rhs, start=start, stop=stop)

    def tr(self, out_psum, in_, ident):
        self.comp("tensor", "transpose", out_psum, in_, ident)

    def act(self, out, in_, func, **kw):
        self.comp("scalar", "activation", out, in_, func, **kw)

    def v(self, meth, *args, **kw):
        self.comp("vector", meth, *args, **kw)

    def g(self, meth, *args, **kw):
        self.comp("gpsimd", meth, *args, **kw)


class Ph:

    def __init__(self, name):
        self.nc = bass.Bass("TRN2", target_bir_lowering=False, debug=False)
        self.name = name
        self.es = ExitStack()
        self.dsem = self.es.enter_context(self.nc.semaphore("dsem"))
        self.csem = self.es.enter_context(self.nc.semaphore("csem"))
        self.r = Rec(self.dsem, self.csem)
        self.IN = {}
        self.OUT = {}
        self._in_arrays = {}
        self.prefix = ""
        self.sec_es = None

    def begin(self, prefix):
        self.prefix = prefix + "_"
        self.sec_es = ExitStack()

    def end(self):
        with self.nc.Block() as block:
            self.r.flush_into(block)
        self.sec_es.close()
        self.sec_es = None
        self.prefix = ""

    def inp(self, name, arrs):
        if isinstance(arrs, np.ndarray):
            arrs = [arrs] * NCORES
        a0 = arrs[0]
        ap = self.nc.dram_tensor(name, list(a0.shape), np2dt(a0.dtype),
                                 kind="ExternalInput").ap()
        self.IN[name] = ap
        self._in_arrays[name] = arrs
        return ap

    def out(self, name, shape, dtype):
        ap = self.nc.dram_tensor(name, list(shape), dtype,
                                 kind="ExternalOutput").ap()
        self.OUT[name] = ap
        return ap

    def dram(self, name, shape, dtype):
        return self.nc.dram_tensor(name, list(shape), dtype, kind="Internal").ap()

    def sb(self, name, shape, dtype):
        es = self.sec_es if self.sec_es is not None else self.es
        return es.enter_context(
            self.nc.sbuf_tensor(self.prefix + name, list(shape), dtype))

    def ps(self, name, shape, dtype=F32):
        es = self.sec_es if self.sec_es is not None else self.es
        return es.enter_context(
            self.nc.psum_tensor(self.prefix + name, list(shape), dtype))

    def run(self, trace=False):
        self.r.finish()
        with self.nc.Block() as block:
            self.r.flush_into(block)
        self.es.close()
        in_maps = [{k: self._in_arrays[k][i] for k in self._in_arrays}
                   for i in range(NCORES)]
        res = run_bass_kernel_spmd(self.nc, in_maps,
                                   core_ids=list(range(NCORES)), trace=trace)
        return res.results



def emit_const_tiles(ph):
    r = ph.r
    c = {}
    c["id_bf"] = ph.sb("id_bf", [128, 128], BF16)
    c["id_f"] = ph.sb("id_f", [128, 128], F32)
    c["tri"] = ph.sb("tri", [128, 128], F32)
    c["ones_r_bf"] = ph.sb("ones_r_bf", [1, 128], BF16)
    c["ones_r_f"] = ph.sb("ones_r_f", [1, 128], F32)
    c["ones_c_bf"] = ph.sb("ones_c_bf", [128, 1], BF16)
    c["ones_c_f"] = ph.sb("ones_c_f", [128, 1], F32)
    for t_ in ("id_bf", "id_f"):
        r.g("memset", c[t_][:], 0.0)
        r.g("affine_select", out=c[t_][:], in_=c[t_][:],
            compare_op=ALU.not_equal, fill=1.0, base=0,
            pattern=[[-1, 128]], channel_multiplier=1)
    r.g("memset", c["tri"][:], 0.0)
    r.g("affine_select", out=c["tri"][:], in_=c["tri"][:],
        compare_op=ALU.is_gt, fill=1.0, base=0,
        pattern=[[-1, 128]], channel_multiplier=1)
    for t_ in ("ones_r_bf", "ones_r_f", "ones_c_bf", "ones_c_f"):
        r.g("memset", c[t_][:], 1.0)
    return c


def emit_ln_tile(ph, r, x_tile, out_ap, scr, eps, gb_tiles=None,
                 negmean_out=None, rstd_out=None):
    sq, s1, s2, s3 = scr["sq"], scr["s1"], scr["s2"], scr["s3"]
    r.v("tensor_reduce", s1[:], x_tile, AX, ALU.add)
    r.v("tensor_scalar_mul", s1[:], s1[:], -1.0 / D)
    r.act(sq[:], x_tile, AF.Square)
    r.v("tensor_reduce", s2[:], sq[:], AX, ALU.add)
    r.v("tensor_scalar_mul", s2[:], s2[:], 1.0 / D)
    r.v("tensor_mul", s3[:], s1[:], s1[:])
    r.v("tensor_sub", s2[:], s2[:], s3[:])
    r.v("tensor_single_scalar", s2[:], s2[:], eps, ALU.add)
    r.act(s2[:], s2[:], AF.Ln)
    r.v("tensor_scalar_mul", s2[:], s2[:], -0.5)
    r.act(s2[:], s2[:], AF.Exp)
    if negmean_out is not None:
        r.v("tensor_copy", negmean_out, s1[:])
    if rstd_out is not None:
        r.v("tensor_copy", rstd_out, s2[:])
    if gb_tiles is None:
        r.v("tensor_scalar", out_ap, x_tile, s1[:], s2[:], ALU.add, ALU.mult)
    else:
        G, B = gb_tiles
        r.v("tensor_scalar", scr["xn"][:], x_tile, s1[:], s2[:], ALU.add, ALU.mult)
        r.v("tensor_mul", scr["xn"][:], scr["xn"][:], G[:])
        r.v("tensor_tensor", out_ap, scr["xn"][:], B[:], ALU.add)


def emit_row_bcast(ph, r, c, row_f32_ap, out_tile, n, pst):
    for off in range(0, n, 512):
        w = min(512, n - off)
        r.mm(pst[:, 0:w], c["ones_r_f"][:], row_f32_ap[0:1, off:off + w])
        r.v("tensor_copy", out_tile[:, off:off + w], pst[:, 0:w])



def emit_ln0(ph):
    r = ph.r
    x = ph.IN["x"]
    xn0t = ph.OUT["XN0T"]
    xt = ph.sb("xt", [128, D], F32)
    xnb = ph.sb("xnb", [128, D], BF16)
    scr = {"sq": ph.sb("sq", [128, D], F32),
           "s1": ph.sb("s1", [128, 1], F32),
           "s2": ph.sb("s2", [128, 1], F32),
           "s3": ph.sb("s3", [128, 1], F32)}
    for tc in range(T // 128):
        r.dma(xt[:], x[tc * 128:(tc + 1) * 128, :])
        emit_ln_tile(ph, r, xt[:], xnb[:], scr, 1e-5)
        r.dma(xn0t[tc * 128:(tc + 1) * 128, :], xnb[:])



def emit_attn0(ph):
    r = ph.r
    XN0T = ph.IN["XN0T"]
    w_qkv = ph.IN["w_qkv"]
    fc_code = ph.IN["fc_code"].rearrange("o h c d -> (o h c) d")
    w_out = ph.IN["w_out"]
    AO = ph.OUT["AO"]

    c = emit_const_tiles(ph)
    tri_bf = ph.sb("tri_bf", [128, 128], BF16)
    r.v("tensor_copy", tri_bf[:], c["tri"][:])

    wqkv_bf = ph.dram("wqkv_bf", [3 * D, D], BF16)
    WQKVT = ph.dram("WQKVT", [D, 3 * D], BF16)
    wout_bf = ph.dram("wout_bf", [D, D], BF16)
    WOUTT = ph.dram("WOUTT", [D, D], BF16)
    QKV = ph.dram("QKV", [3 * D, T], BF16)
    r.dma(wqkv_bf, w_qkv, eng="gpsimd")
    r.dma(wout_bf, w_out, eng="gpsimd")
    tbig = ph.sb("tbig", [128, 3 * D], BF16)
    for k in range(8):
        r.dma(tbig[:], wqkv_bf[0:3 * D, k * 128:(k + 1) * 128], transpose=True)
        r.dma(WQKVT[k * 128:(k + 1) * 128, :], tbig[:])
    for k in range(8):
        r.dma(tbig[:, 0:D], wout_bf[0:D, k * 128:(k + 1) * 128], transpose=True)
        r.dma(WOUTT[k * 128:(k + 1) * 128, :], tbig[:, 0:D])

    ps_tr = ph.ps("ps_tr", [128, 1024], BF16)
    ps_a = ph.ps("ps_a", [128, 512], F32)
    ps_qc = ph.ps("ps_qc", [128, 512], F32)
    ps_cum = ph.ps("ps_cum", [128, 512], F32)
    ps_xo = ph.ps("ps_xo", [128, 512], F32)
    ps_M = ph.ps("ps_M", [128, 512], F32)
    ps_car = ph.ps("ps_car", [128, 512], F32)

    code_nat = ph.sb("code_nat", [16, 64], BF16)
    codeT = ph.sb("codeT", [64, 16 * NH], BF16)
    for h in range(NH):
        r.dma(code_nat[:], fc_code[h * 16:(h + 1) * 16, :], eng="gpsimd")
        r.tr(ps_tr[0:64, 0:16], code_nat[:], c["id_bf"][0:16, 0:16])
        r.v("tensor_scalar_mul", codeT[:, h * 16:(h + 1) * 16],
            ps_tr[0:64, 0:16], 0.125)

    xnf = [ph.sb(f"xnf{k}", [128, S], BF16) for k in range(8)]
    wm = ph.sb("wm", [128, 8 * 128], BF16)
    qkv_t = ph.sb("qkv_t", [128, S], BF16)
    attn_fm = [ph.sb(f"afm{k}", [128, S], BF16) for k in range(8)]
    woutt_sb = ph.sb("woutt", [128, 8 * D], BF16)
    ao_t = ph.sb("ao_t", [128, D], F32)
    for k in range(8):
        r.dma(woutt_sb[:, k * D:(k + 1) * D], WOUTT[k * 128:(k + 1) * 128, :])

    qf = ph.sb("qf", [64, S], BF16)
    kf = ph.sb("kf", [64, S], BF16)
    vf = ph.sb("vf", [64, S], BF16)
    kcm = ph.sb("kcm", [16, S], F32)
    kexp_bf = ph.sb("kexp_bf", [16, S], BF16)
    nmax = ph.sb("nmax", [16, 1], F32)
    S_tm = ph.sb("S_tm", [128, 16], BF16)
    V_tm = ph.sb("V_tm", [128, 64], BF16)
    qc = ph.sb("qc", [128, 16], F32)
    q2b = ph.sb("q2b", [128, 16], BF16)
    q2T = ph.sb("q2T", [16, 128], BF16)
    Am = ph.sb("Am", [128, 128], BF16)
    kc_sb = ph.sb("kc_sb", [128, 16], F32)
    dden = ph.sb("dden", [128, 16], F32)
    M_sb = ph.sb("M_sb", [16, 64], BF16)
    carry_bf = ph.sb("carry_bf", [1, 16], BF16)
    smax = ph.sb("smax", [128, 1], F32)
    ssum = ph.sb("ssum", [128, 1], F32)
    srec = ph.sb("srec", [128, 1], F32)

    for s in range(NSEQ):
        srow = s * S
        for k in range(8):
            r.dma(xnf[k][:], XN0T[srow:srow + S, k * 128:(k + 1) * 128],
                  transpose=True)
        for m in range(24):
            r.dma(wm[:].rearrange("p (k o) -> p k o", k=8),
                  WQKVT[:, m * 128:(m + 1) * 128].rearrange("(k p) o -> p k o", p=128))
            for n in range(4):
                for k in range(8):
                    r.mm(ps_a[:, 0:512], wm[:, k * 128:(k + 1) * 128],
                         xnf[k][:, n * 512:(n + 1) * 512],
                         start=(k == 0), stop=(k == 7))
                r.v("tensor_copy", qkv_t[:, n * 512:(n + 1) * 512], ps_a[:, 0:512])
            r.dma(QKV[m * 128:(m + 1) * 128, srow:srow + S], qkv_t[:])

        for h in range(NH):
            r.dma(qf[:], QKV[h * HD:(h + 1) * HD, srow:srow + S])
            r.dma(kf[:], QKV[D + h * HD:D + (h + 1) * HD, srow:srow + S])
            r.dma(vf[:], QKV[2 * D + h * HD:2 * D + (h + 1) * HD, srow:srow + S])
            ct = codeT[:, h * 16:(h + 1) * 16]
            for n in range(4):
                r.mm(ps_a[0:16, 0:512], ct, kf[:, n * 512:(n + 1) * 512])
                r.v("tensor_copy", kcm[:, n * 512:(n + 1) * 512], ps_a[0:16, 0:512])
            r.v("reduce_max", nmax[:], kcm[:], AX)
            r.v("tensor_scalar_mul", nmax[:], nmax[:], -1.0)
            r.act(kcm[:], kcm[:], AF.Exp, bias=nmax[:])
            r.v("tensor_copy", kexp_bf[:], kcm[:])
            for j in range(S // 128):
                tok = slice(j * 128, (j + 1) * 128)
                last = (j == S // 128 - 1)
                if j == 0:
                    r.g("memset", M_sb[:], 0.0)
                    r.g("memset", carry_bf[:], 0.0)
                else:
                    r.v("tensor_copy", M_sb[:], ps_M[0:16, 0:64])
                    r.v("tensor_copy", carry_bf[:], ps_car[0:1, 0:16])
                r.tr(ps_tr[0:128, 0:16], kexp_bf[:, tok], c["id_bf"][0:16, 0:16])
                r.v("tensor_copy", S_tm[:], ps_tr[0:128, 0:16])
                r.tr(ps_tr[0:128, 128:192], vf[:, tok], c["id_bf"][0:64, 0:64])
                r.v("tensor_copy", V_tm[:], ps_tr[0:128, 128:192])
                r.mm(ps_cum[0:128, 0:16], tri_bf[:], S_tm[:], start=True, stop=False)
                r.mm(ps_cum[0:128, 0:16], c["ones_r_bf"][:], carry_bf[:],
                     start=False, stop=True)
                r.v("tensor_single_scalar", kc_sb[:], ps_cum[0:128, 0:16], 1e-9, ALU.add)
                r.v("reciprocal", dden[:], kc_sb[:])
                r.mm(ps_qc[0:128, 0:16], qf[:, tok], ct)
                r.v("reduce_max", smax[:], ps_qc[0:128, 0:16], AX)
                r.v("tensor_scalar_mul", smax[:], smax[:], -1.0)
                r.act(qc[:], ps_qc[0:128, 0:16], AF.Exp, bias=smax[:])
                r.v("tensor_reduce", ssum[:], qc[:], AX, ALU.add)
                r.v("reciprocal", srec[:], ssum[:])
                r.v("tensor_scalar", qc[:], qc[:], srec[:], 0.125, ALU.mult, ALU.mult)
                r.v("tensor_tensor", q2b[:], qc[:], dden[:], ALU.mult)
                r.tr(ps_tr[0:16, 256:384], q2b[:], c["id_bf"][:])
                r.v("tensor_copy", q2T[:], ps_tr[0:16, 256:384])
                r.mm(ps_a[0:128, 0:128], kexp_bf[:, tok], q2T[:])
                r.v("tensor_tensor", Am[:], ps_a[0:128, 0:128], c["tri"][:], ALU.mult)
                r.mm(ps_xo[0:64, 0:128], M_sb[:], q2T[:], start=True, stop=False)
                r.mm(ps_xo[0:64, 0:128], V_tm[:], Am[:], start=False, stop=True)
                poff = 64 * (h % 2)
                r.v("tensor_copy", attn_fm[h // 2][poff:poff + 64, tok],
                    ps_xo[0:64, 0:128])
                r.mm(ps_M[0:16, 0:64], S_tm[:], V_tm[:], start=(j == 0), stop=last)
                r.mm(ps_car[0:1, 0:16], c["ones_c_bf"][:], S_tm[:], start=(j == 0), stop=last)

        for tc in range(S // 128):
            for n in range(2):
                for k in range(8):
                    r.mm(ps_a[:, 0:512], attn_fm[k][:, tc * 128:(tc + 1) * 128],
                         woutt_sb[:, k * D + n * 512:k * D + (n + 1) * 512],
                         start=(k == 0), stop=(k == 7))
                r.v("tensor_copy", ao_t[:, n * 512:(n + 1) * 512], ps_a[:, 0:512])
            r.dma(AO[srow + tc * 128:srow + (tc + 1) * 128, :], ao_t[:])



def emit_wprep_T(ph, r, src_ap, rows, cols, name):
    wbf = ph.dram(name + "_bf", [rows, cols], BF16)
    wT = ph.dram(name + "_T", [cols, rows], BF16)
    r.dma(wbf, src_ap, eng="gpsimd")
    tb = ph.sb(name + "_tb", [128, rows], BF16)
    for k in range(cols // 128):
        r.dma(tb[:], wbf[0:rows, k * 128:(k + 1) * 128], transpose=True)
        r.dma(wT[k * 128:(k + 1) * 128, :], tb[:])
    return wT


def emit_scale_cols(ph, r, c, s_ap, n, name, pst):
    nm = n // 128
    tmp = ph.sb(name + "_ld", [nm, 128], F32)
    out = ph.sb(name + "_c", [128, nm], F32)
    r.dma(tmp[:], s_ap.rearrange("(m p) -> m p", p=128))
    r.tr(pst[0:128, 0:nm], tmp[:], c["id_f"][0:nm, 0:nm])
    r.v("tensor_copy", out[:], pst[0:128, 0:nm])
    return out


def emit_ffn0a(ph):
    r = ph.r
    XN0T = ph.IN["XN0T"]
    H = ph.OUT["H"]
    c = emit_const_tiles(ph)
    pscr = ph.ps("pscr", [128, 512], F32)
    W1T = emit_wprep_T(ph, r, ph.IN["w1_w"], DH, D, "w1")
    W2T = emit_wprep_T(ph, r, ph.IN["w2_w"], DH, D, "w2")
    s1c = emit_scale_cols(ph, r, c, ph.IN["w1_s"], DH, "s1", pscr)
    s2c = emit_scale_cols(ph, r, c, ph.IN["w2_s"], DH, "s2", pscr)

    xnf = [ph.sb(f"xnf{k}", [128, T], BF16) for k in range(8)]
    for k in range(8):
        r.dma(xnf[k][:], XN0T[0:T, k * 128:(k + 1) * 128], transpose=True)
    w1m = ph.sb("w1m", [128, 8 * 128], BF16)
    w2m = ph.sb("w2m", [128, 8 * 128], BF16)
    tsil = ph.sb("tsil", [128, 512], F32)
    tmul = ph.sb("tmul", [128, 512], F32)
    h_row = ph.sb("h_row", [128, T], BF16)
    ps1 = ph.ps("ps1", [128, 512], F32)
    ps2 = ph.ps("ps2", [128, 512], F32)
    for m in range(DH // 128):
        r.dma(w1m[:].rearrange("p (k o) -> p k o", k=8),
              W1T[:, m * 128:(m + 1) * 128].rearrange("(k p) o -> p k o", p=128))
        r.dma(w2m[:].rearrange("p (k o) -> p k o", k=8),
              W2T[:, m * 128:(m + 1) * 128].rearrange("(k p) o -> p k o", p=128))
        for n in range(T // 512):
            nt = slice(n * 512, (n + 1) * 512)
            for k in range(8):
                r.mm(ps1[:], w1m[:, k * 128:(k + 1) * 128], xnf[k][:, nt],
                     start=(k == 0), stop=(k == 7))
            for k in range(8):
                r.mm(ps2[:], w2m[:, k * 128:(k + 1) * 128], xnf[k][:, nt],
                     start=(k == 0), stop=(k == 7))
            r.act(tsil[:], ps1[:], AF.Silu, scale=s1c[:, m:m + 1])
            r.v("tensor_scalar_mul", tmul[:], ps2[:], s2c[:, m:m + 1])
            r.v("tensor_tensor", h_row[:, nt], tsil[:], tmul[:], ALU.mult)
        r.dma(H[m * 128:(m + 1) * 128, :], h_row[:])


def emit_ffn0bc(ph, win, wsn, inname, outname):
    r = ph.r
    HI = ph.IN[inname]
    GO = ph.OUT[outname]
    c = emit_const_tiles(ph)
    pscr = ph.ps("pscr", [128, 512], F32)
    WT = emit_wprep_T(ph, r, ph.IN[win], DH, DH, win)
    sc = emit_scale_cols(ph, r, c, ph.IN[wsn], DH, wsn, pscr)

    PAN = 2048
    hp = [ph.sb(f"hp{k}", [128, PAN], BF16) for k in range(DH // 128)]
    wtm = ph.sb("wtm", [128, (DH // 128) * 128], BF16)
    g_row = ph.sb("g_row", [128, PAN], BF16)
    psg = ph.ps("psg", [128, 512], F32)
    nk = DH // 128
    for p in range(T // PAN):
        pc = slice(p * PAN, (p + 1) * PAN)
        for k in range(nk):
            r.dma(hp[k][:], HI[k * 128:(k + 1) * 128, pc])
        for m in range(nk):
            r.dma(wtm[:].rearrange("p (k o) -> p k o", k=nk),
                  WT[:, m * 128:(m + 1) * 128].rearrange("(k p) o -> p k o", p=128))
            for n in range(PAN // 512):
                nt = slice(n * 512, (n + 1) * 512)
                for k in range(nk):
                    r.mm(psg[:], wtm[:, k * 128:(k + 1) * 128], hp[k][:, nt],
                         start=(k == 0), stop=(k == nk - 1))
                r.act(g_row[:, nt], psg[:], AF.Gelu, scale=sc[:, m:m + 1])
            r.dma(GO[m * 128:(m + 1) * 128, pc], g_row[:])


def emit_ffn0d(ph):
    r = ph.r
    G2 = ph.IN["G2"]
    x = ph.IN["x"]
    AO = ph.IN["AO"]
    X1 = ph.OUT["X1"]
    c = emit_const_tiles(ph)
    pscr = ph.ps("pscr", [128, 512], F32)
    W3T = emit_wprep_T(ph, r, ph.IN["w3_w"], D, DH, "w3")

    w3sb = ph.sb("w3sb", [128, (DH // 128) * D], BF16)
    for k in range(DH // 128):
        r.dma(w3sb[:, k * D:(k + 1) * D], W3T[k * 128:(k + 1) * 128, :])
    s3row = ph.sb("s3row", [1, D], F32)
    r.dma(s3row[:], ph.IN["w3_s"].rearrange("(o d) -> o d", o=1))
    W3S = ph.sb("W3S", [128, D], F32)
    emit_row_bcast(ph, r, c, s3row[:], W3S, D, pscr)

    g2m = ph.sb("g2m", [128, (DH // 128) * 128], BF16)
    yt = ph.sb("yt", [128, D], F32)
    xt = ph.sb("xt", [128, D], F32)
    aot = ph.sb("aot", [128, D], F32)
    psd = ph.ps("psd", [128, 512], F32)
    nk = DH // 128
    for tc in range(T // 128):
        r.dma(g2m[:].rearrange("p (k t) -> p k t", k=nk),
              G2[:, tc * 128:(tc + 1) * 128].rearrange("(k p) t -> p k t", p=128))
        r.dma(xt[:], x[tc * 128:(tc + 1) * 128, :])
        r.dma(aot[:], AO[tc * 128:(tc + 1) * 128, :])
        for n in range(2):
            nt = slice(n * 512, (n + 1) * 512)
            for k in range(nk):
                r.mm(psd[:], g2m[:, k * 128:(k + 1) * 128],
                     w3sb[:, k * D + n * 512:k * D + (n + 1) * 512],
                     start=(k == 0), stop=(k == nk - 1))
            r.v("tensor_tensor", yt[:, nt], psd[:], W3S[:, nt], ALU.mult)
        r.v("tensor_tensor", yt[:], yt[:], xt[:], ALU.add)
        r.v("tensor_tensor", yt[:], yt[:], aot[:], ALU.add)
        r.dma(X1[tc * 128:(tc + 1) * 128, :], yt[:])



def emit_l1pre(ph):
    r = ph.r
    X1 = ph.IN["X1"]
    XN1 = ph.OUT["XN1"]
    Ef = ph.OUT["E"]
    V1f = ph.OUT["V1"]
    Rf = ph.OUT["R"]
    c = emit_const_tiles(ph)

    g_row = ph.sb("g_row", [1, D], F32)
    b_row = ph.sb("b_row", [1, D], F32)
    r.dma(g_row[:], ph.IN["ln1_g"].rearrange("(o d) -> o d", o=1))
    r.dma(b_row[:], ph.IN["ln1_b"].rearrange("(o d) -> o d", o=1))
    pscr = ph.ps("pscr", [128, 512], F32)
    G1t = ph.sb("G1t", [128, D], F32)
    B1t = ph.sb("B1t", [128, D], F32)
    emit_row_bcast(ph, r, c, g_row[:], G1t, D, pscr)
    emit_row_bcast(ph, r, c, b_row[:], B1t, D, pscr)

    A_lnT = ph.dram("A_lnT", [T, D], BF16)
    xt = ph.sb("xt", [128, D], F32)
    xn1t = ph.sb("xn1t", [128, D], F32)
    abf = ph.sb("abf", [128, D], BF16)
    scr = {"sq": ph.sb("sq", [128, D], F32),
           "s1": ph.sb("s1", [128, 1], F32),
           "s2": ph.sb("s2", [128, 1], F32),
           "s3": ph.sb("s3", [128, 1], F32),
           "xn": ph.sb("xnscr", [128, D], F32)}
    for tc in range(T // 128):
        rows = slice(tc * 128, (tc + 1) * 128)
        r.dma(xt[:], X1[rows, :])
        emit_ln_tile(ph, r, xt[:], xn1t[:], scr, 1e-5)
        r.dma(XN1[rows, :], xn1t[:])
        emit_ln_tile(ph, r, xn1t[:], abf[:], scr, 1e-5, gb_tiles=(G1t, B1t))
        r.dma(A_lnT[rows, :], abf[:])

    muk = emit_scale_cols(ph, r, c, ph.IN["tm_mu_k"], D, "muk", pscr)
    muv = emit_scale_cols(ph, r, c, ph.IN["tm_mu_v"], D, "muv", pscr)
    mur = emit_scale_cols(ph, r, c, ph.IN["tm_mu_r"], D, "mur", pscr)

    wk_bf = ph.dram("wk_bf", [D, D], BF16)
    wv_bf = ph.dram("wv_bf", [D, D], BF16)
    wr_bf = ph.dram("wr_bf", [D, D], BF16)
    r.dma(wk_bf, ph.IN["Wk"], eng="gpsimd")
    r.dma(wv_bf, ph.IN["Wv"], eng="gpsimd")
    r.dma(wr_bf, ph.IN["Wr"], eng="gpsimd")

    xsf = [ph.sb(f"xsf{k}", [128, T], BF16) for k in range(8)]
    abuf = ph.sb("abuf", [128, S + 16], BF16)
    dbuf = ph.sb("dbuf", [128, S], BF16)
    wsb = ph.sb("wsb", [128, 8 * D], BF16)
    erow = ph.sb("erow", [128, T], F32)
    psp = ph.ps("psp", [128, 512], F32)
    r.g("memset", abuf[:, 15:16], 0.0)

    for st, (mu, wbf, outdram) in enumerate(
            [(muk, wk_bf, Ef), (muv, wv_bf, V1f), (mur, wr_bf, Rf)]):
        for cc in range(8):
            for s in range(NSEQ):
                r.dma(abuf[:, 16:16 + S],
                      A_lnT[s * S:(s + 1) * S, cc * 128:(cc + 1) * 128],
                      transpose=True)
                a_sl = abuf[:, 16:16 + S]
                xx_sl = abuf[:, 15:15 + S]
                r.v("tensor_tensor", dbuf[:], a_sl, xx_sl, ALU.subtract)
                r.v("scalar_tensor_tensor", xsf[cc][:, s * S:(s + 1) * S],
                    dbuf[:], mu[:, cc:cc + 1], xx_sl, ALU.mult, ALU.add)
        for k in range(8):
            r.dma(wsb[:, k * D:(k + 1) * D], wbf[k * 128:(k + 1) * 128, :])
        for m in range(8):
            for n in range(T // 512):
                nt = slice(n * 512, (n + 1) * 512)
                for k in range(8):
                    r.mm(psp[:], wsb[:, k * D + m * 128:k * D + (m + 1) * 128],
                         xsf[k][:, nt], start=(k == 0), stop=(k == 7))
                if st == 0:
                    r.act(erow[:, nt], psp[:], AF.Exp)
                elif st == 1:
                    r.v("tensor_copy", erow[:, nt], psp[:])
                else:
                    r.act(erow[:, nt], psp[:], AF.Sigmoid)
            r.dma(outdram[m * 128:(m + 1) * 128, :], erow[:])


def emit_scan(ph):
    r = ph.r
    Ef = ph.IN["E"]
    V1f = ph.IN["V1"]
    Rf = ph.IN["R"]
    XN1 = ph.IN["XN1"]
    X1 = ph.IN["X1"]
    ACC = ph.OUT["ACC"]
    EIT = ph.OUT["EIT"]
    c = emit_const_tiles(ph)

    pscr = ph.ps("pscr", [128, 512], F32)
    tdc = emit_scale_cols(ph, r, c, ph.IN["time_decay"], D, "tdc", pscr)
    tfc = emit_scale_cols(ph, r, c, ph.IN["time_first"], D, "tfc", pscr)
    lam = ph.sb("lam", [128, 8], F32)
    eu = ph.sb("eu", [128, 8], F32)
    r.act(lam[:], tdc[:], AF.Exp)
    r.v("tensor_scalar_mul", lam[:], lam[:], -1.0)
    r.act(lam[:], lam[:], AF.Exp)
    r.act(eu[:], tfc[:], AF.Exp)

    wo_bf = ph.dram("wo_bf", [D, D], BF16)
    r.dma(wo_bf, ph.IN["Wo"], eng="gpsimd")

    ec = ph.sb("ec", [128, S], F32)
    vc = ph.sb("vc", [128, S], F32)
    rc = ph.sb("rc", [128, S], F32)
    za = ph.sb("za", [128, S], F32)
    Ab = ph.sb("Ab", [128, S + 16], F32)
    Bb = ph.sb("Bb", [128, S + 16], F32)
    ek = ph.sb("ek", [128, S], F32)
    nb = ph.sb("nb", [128, S], F32)
    db = ph.sb("db", [128, S], F32)
    RW = [ph.sb(f"RW{k}", [128, T], BF16) for k in range(8)]
    r.g("memset", Ab[:, 15:16], 0.0)
    r.g("memset", Bb[:, 15:16], 0.0)

    for cc in range(8):
        rows = slice(cc * 128, (cc + 1) * 128)
        for s in range(NSEQ):
            cols = slice(s * S, (s + 1) * S)
            r.dma(ec[:], Ef[rows, cols])
            r.dma(vc[:], V1f[rows, cols])
            r.dma(rc[:], Rf[rows, cols])
            lamb = lam[:, cc:cc + 1].broadcast_to([128, S])
            r.v("tensor_tensor", za[:], ec[:], vc[:], ALU.mult)
            r.v("tensor_tensor_scan", Ab[:, 16:16 + S], lamb, za[:], 0.0,
                ALU.mult, ALU.add)
            r.v("tensor_tensor_scan", Bb[:, 16:16 + S], lamb, ec[:], 0.0,
                ALU.mult, ALU.add)
            r.v("tensor_scalar_mul", ek[:], ec[:], eu[:, cc:cc + 1])
            r.v("tensor_tensor", nb[:], ek[:], vc[:], ALU.mult)
            r.v("tensor_tensor", nb[:], nb[:], Ab[:, 15:15 + S], ALU.add)
            r.v("tensor_tensor", db[:], ek[:], Bb[:, 15:15 + S], ALU.add)
            r.v("reciprocal", db[:], db[:])
            r.v("tensor_tensor", nb[:], nb[:], db[:], ALU.mult)
            r.v("tensor_tensor", RW[cc][:, cols], nb[:], rc[:], ALU.mult)

    wosb = ph.sb("wosb", [128, 8 * D], BF16)
    for k in range(8):
        r.dma(wosb[:, k * D:(k + 1) * D], wo_bf[k * 128:(k + 1) * 128, :])
    g2_row = ph.sb("g2_row", [1, D], F32)
    b2_row = ph.sb("b2_row", [1, D], F32)
    r.dma(g2_row[:], ph.IN["ln2_g"].rearrange("(o d) -> o d", o=1))
    r.dma(b2_row[:], ph.IN["ln2_b"].rearrange("(o d) -> o d", o=1))
    G2t = ph.sb("G2t", [128, D], F32)
    B2t = ph.sb("B2t", [128, D], F32)
    emit_row_bcast(ph, r, c, g2_row[:], G2t, D, pscr)
    emit_row_bcast(ph, r, c, b2_row[:], B2t, D, pscr)

    psw = ph.ps("psw", [128, 512], F32)
    yt = ph.sb("yt", [128, D], F32)
    xnt = ph.sb("xnt", [128, D], F32)
    x1t = ph.sb("x1t", [128, D], F32)
    eib = ph.sb("eib", [128, D], BF16)
    scr = {"sq": ph.sb("sq", [128, D], F32),
           "s1": ph.sb("s1", [128, 1], F32),
           "s2": ph.sb("s2", [128, 1], F32),
           "s3": ph.sb("s3", [128, 1], F32),
           "xn": ph.sb("xnscr", [128, D], F32)}
    for tcg in range(T // 128):
        rows = slice(tcg * 128, (tcg + 1) * 128)
        s = tcg // (S // 128)
        tc = tcg % (S // 128)
        for n in range(2):
            nt = slice(n * 512, (n + 1) * 512)
            for k in range(8):
                r.mm(psw[:], RW[k][:, tcg * 128:(tcg + 1) * 128],
                     wosb[:, k * D + n * 512:k * D + (n + 1) * 512],
                     start=(k == 0), stop=(k == 7))
            r.v("tensor_copy", yt[:, nt], psw[:])
        r.dma(xnt[:], XN1[rows, :])
        r.dma(x1t[:], X1[rows, :])
        r.v("tensor_tensor", yt[:], yt[:], xnt[:], ALU.add)
        emit_ln_tile(ph, r, yt[:], eib[:], scr, 1e-5, gb_tiles=(G2t, B2t))
        r.dma(EIT[rows, :], eib[:])
        r.v("tensor_tensor", yt[:], yt[:], x1t[:], ALU.add)
        r.dma(ACC[rows, :], yt[:])


def emit_router(ph):
    r = ph.r
    EIT = ph.IN["EIT"]
    TW = ph.OUT["TW"]
    rw_bf = ph.dram("rw_bf", [D, NE], BF16)
    r.dma(rw_bf, ph.IN["router_w"], eng="gpsimd")
    rwsb = ph.sb("rwsb", [128, 8 * NE], BF16)
    for k in range(8):
        r.dma(rwsb[:, k * NE:(k + 1) * NE], rw_bf[k * 128:(k + 1) * 128, :])
    eif = [ph.sb(f"eif{k}", [128, T], BF16) for k in range(8)]
    for k in range(8):
        r.dma(eif[k][:], EIT[0:T, k * 128:(k + 1) * 128], transpose=True)
    psr = ph.ps("psr", [128, 512], F32)
    pt = ph.sb("pt", [128, NE], F32)
    p2 = ph.sb("p2", [128, NE], F32)
    twt = ph.sb("twt", [128, NE], F32)
    nm = ph.sb("nm", [128, 1], F32)
    sm = ph.sb("sm", [128, 1], F32)
    v1 = ph.sb("v1", [128, 1], F32)
    v2 = ph.sb("v2", [128, 1], F32)
    for tc in range(T // 128):
        for k in range(8):
            r.mm(psr[:, 0:NE], eif[k][:, tc * 128:(tc + 1) * 128],
                 rwsb[:, k * NE:(k + 1) * NE], start=(k == 0), stop=(k == 7))
        r.v("reduce_max", nm[:], psr[:, 0:NE], AX)
        r.v("tensor_scalar_mul", nm[:], nm[:], -1.0)
        r.act(pt[:], psr[:, 0:NE], AF.Exp, bias=nm[:])
        r.v("tensor_reduce", sm[:], pt[:], AX, ALU.add)
        r.v("reciprocal", sm[:], sm[:])
        r.v("tensor_scalar_mul", pt[:], pt[:], sm[:])
        r.v("reduce_max", v1[:], pt[:], AX)
        r.v("tensor_single_scalar", p2[:], pt[:], v1[:], ALU.is_ge)
        r.v("scalar_tensor_tensor", p2[:], p2[:], -1e30, pt[:], ALU.mult, ALU.add)
        r.v("reduce_max", v2[:], p2[:], AX)
        r.v("tensor_tensor", sm[:], v1[:], v2[:], ALU.add)
        r.v("reciprocal", sm[:], sm[:])
        r.v("tensor_single_scalar", twt[:], pt[:], v2[:], ALU.is_ge)
        r.v("tensor_tensor", twt[:], twt[:], pt[:], ALU.mult)
        r.v("tensor_scalar_mul", twt[:], twt[:], sm[:])
        r.dma(TW[tc * 128:(tc + 1) * 128, :], twt[:])


def emit_moe(ph):
    r = ph.r
    EIT = ph.IN["EIT"]
    TW = ph.IN["TW"]
    MOE = ph.OUT["MOE"]
    c = emit_const_tiles(ph)

    ewk_bf = ph.dram("ewk_bf", [NE * D, FE], BF16)
    ewv_bf = ph.dram("ewv_bf", [NE * FE, D], BF16)
    ewr_bf = ph.dram("ewr_bf", [NE * D, D], BF16)
    r.dma(ewk_bf, ph.IN["eWk"].rearrange("e d f -> (e d) f"), eng="gpsimd")
    r.dma(ewv_bf, ph.IN["eWv"].rearrange("e f d -> (e f) d"), eng="gpsimd")
    r.dma(ewr_bf, ph.IN["eWr"].rearrange("e d g -> (e d) g"), eng="gpsimd")

    cmk_flat = ph.IN["cm_mu_k"].rearrange("e d -> (e d)")
    cmr_flat = ph.IN["cm_mu_r"].rearrange("e d -> (e d)")
    pscr = ph.ps("pscr", [128, 512], F32)
    muks = [emit_scale_cols(ph, r, c, cmk_flat[e * D:(e + 1) * D], D, f"cmk{e}", pscr)
            for e in range(NE)]
    murs = [emit_scale_cols(ph, r, c, cmr_flat[e * D:(e + 1) * D], D, f"cmr{e}", pscr)
            for e in range(NE)]

    EXXf = ph.dram("EXXf", [D, T], BF16)
    DFMf = ph.dram("DFMf", [D, T], BF16)
    abuf = ph.sb("abuf", [128, S + 16], BF16)
    dtmp = ph.sb("dtmp", [128, S], BF16)
    r.g("memset", abuf[:, 15:16], 0.0)
    for cc in range(8):
        for s in range(NSEQ):
            r.dma(abuf[:, 16:16 + S],
                  EIT[s * S:(s + 1) * S, cc * 128:(cc + 1) * 128], transpose=True)
            r.dma(EXXf[cc * 128:(cc + 1) * 128, s * S:(s + 1) * S],
                  abuf[:, 15:15 + S])
            r.v("tensor_tensor", dtmp[:], abuf[:, 16:16 + S], abuf[:, 15:15 + S],
                ALU.subtract)
            r.dma(DFMf[cc * 128:(cc + 1) * 128, s * S:(s + 1) * S], dtmp[:])

    twsb = ph.sb("twsb", [128, (T // 128) * NE], F32)
    for tc in range(T // 128):
        r.dma(twsb[:, tc * NE:(tc + 1) * NE], TW[tc * 128:(tc + 1) * 128, :])

    xke = [ph.sb(f"xke{k}", [128, S], BF16) for k in range(8)]
    ke = [ph.sb(f"ke{k}", [128, S], BF16) for k in range(FE // 128)]
    exx_t = ph.sb("exx_t", [128, S], BF16)
    dfm_t = ph.sb("dfm_t", [128, S], BF16)
    wkm = ph.sb("wkm", [128, 8 * 128], BF16)
    wvsb = ph.sb("wvsb", [128, (FE // 128) * D], BF16)
    wrsb = ph.sb("wrsb", [128, 8 * D], BF16)
    relu_t = ph.sb("relu_t", [128, 512], F32)
    sig_t = ph.sb("sig_t", [128, 512], F32)
    kvr_t = ph.sb("kvr_t", [128, 512], F32)
    out_t = ph.sb("out_t", [128, 512], F32)
    ps_ke = ph.ps("ps_ke", [128, 512], F32)
    ps_r = ph.ps("ps_r", [128, 512], F32)
    ps_kv = ph.ps("ps_kv", [128, 512], F32)

    for e in range(NE):
        for s in range(NSEQ):
            cols = slice(s * S, (s + 1) * S)
            for k in range(8):
                r.dma(exx_t[:], EXXf[k * 128:(k + 1) * 128, cols])
                r.dma(dfm_t[:], DFMf[k * 128:(k + 1) * 128, cols])
                r.v("scalar_tensor_tensor", xke[k][:], dfm_t[:],
                    muks[e][:, k:k + 1], exx_t[:], ALU.mult, ALU.add)
            for m in range(FE // 128):
                r.dma(wkm[:].rearrange("p (k o) -> p k o", k=8),
                      ewk_bf[e * D:(e + 1) * D, m * 128:(m + 1) * 128]
                      .rearrange("(k p) o -> p k o", p=128))
                for n in range(S // 512):
                    nt = slice(n * 512, (n + 1) * 512)
                    for k in range(8):
                        r.mm(ps_ke[:], wkm[:, k * 128:(k + 1) * 128],
                             xke[k][:, nt], start=(k == 0), stop=(k == 7))
                    r.act(relu_t[:], ps_ke[:], AF.Relu)
                    r.v("tensor_tensor", ke[m][:, nt], relu_t[:], relu_t[:],
                        ALU.mult)
            for k in range(8):
                r.dma(exx_t[:], EXXf[k * 128:(k + 1) * 128, cols])
                r.dma(dfm_t[:], DFMf[k * 128:(k + 1) * 128, cols])
                r.v("scalar_tensor_tensor", xke[k][:], dfm_t[:],
                    murs[e][:, k:k + 1], exx_t[:], ALU.mult, ALU.add)
            for k in range(FE // 128):
                r.dma(wvsb[:, k * D:(k + 1) * D],
                      ewv_bf[e * FE + k * 128:e * FE + (k + 1) * 128, :])
            for k in range(8):
                r.dma(wrsb[:, k * D:(k + 1) * D],
                      ewr_bf[e * D + k * 128:e * D + (k + 1) * 128, :])
            for tc in range(S // 128):
                tcg = s * (S // 128) + tc
                tsl = slice(tc * 128, (tc + 1) * 128)
                for n in range(2):
                    for k in range(8):
                        r.mm(ps_r[:], xke[k][:, tsl],
                             wrsb[:, k * D + n * 512:k * D + (n + 1) * 512],
                             start=(k == 0), stop=(k == 7))
                    r.act(sig_t[:], ps_r[:], AF.Sigmoid)
                    for k in range(FE // 128):
                        r.mm(ps_kv[:], ke[k][:, tsl],
                             wvsb[:, k * D + n * 512:k * D + (n + 1) * 512],
                             start=(k == 0), stop=(k == FE // 128 - 1))
                    r.v("tensor_tensor", kvr_t[:], sig_t[:], ps_kv[:], ALU.mult)
                    r.v("tensor_scalar_mul", out_t[:], kvr_t[:],
                        twsb[:, tcg * NE + e:tcg * NE + e + 1])
                    if e == 0:
                        r.dma(MOE[tcg * 128:(tcg + 1) * 128,
                                  n * 512:(n + 1) * 512], out_t[:], eng="gpsimd")
                    else:
                        r.dma(MOE[tcg * 128:(tcg + 1) * 128,
                                  n * 512:(n + 1) * 512], out_t[:], eng="gpsimd",
                              accum_op=ALU.add)


def emit_ffn1(ph):
    r = ph.r
    XN1 = ph.IN["XN1"]
    ACC = ph.IN["ACC"]
    MOE = ph.IN["MOE"]
    OUT = ph.OUT["OUT"]
    c = emit_const_tiles(ph)

    wsb = ph.sb("wsb", [128, 8 * D], F32)
    for k in range(8):
        r.dma(wsb[:, k * D:(k + 1) * D],
              ph.IN["weight"][k * 128:(k + 1) * 128, :])
    asum = ph.sb("asum", [128, 1], F32)
    at = ph.sb("at", [128, 1], F32)
    r.v("tensor_reduce", asum[:], wsb[:, 0:D], AX, ALU.add, True)
    for k in range(1, 8):
        r.v("tensor_reduce", at[:], wsb[:, k * D:(k + 1) * D], AX, ALU.add, True)
        r.v("tensor_tensor", asum[:], asum[:], at[:], ALU.add)
    ps_s = ph.ps("ps_s", [128, 512], F32)
    r.mm(ps_s[0:1, 0:1], c["ones_c_f"][:], asum[:])
    beta11 = ph.sb("beta11", [1, 1], F32)
    r.v("tensor_scalar_mul", beta11[:], ps_s[0:1, 0:1], 1.0 / (D * D))
    beta128 = ph.sb("beta128", [128, 1], F32)
    r.mm(ps_s[0:128, 0:1], c["ones_r_f"][:], beta11[:])
    r.v("tensor_copy", beta128[:], ps_s[0:128, 0:1])
    brec = ph.sb("brec", [128, 1], F32)
    r.v("tensor_single_scalar", brec[:], beta128[:], 1e-5, ALU.add)
    r.v("reciprocal", brec[:], brec[:])
    wqsb = ph.sb("wqsb", [128, 8 * D], BF16)
    zt = ph.sb("zt", [128, D], F32)
    m1t = ph.sb("m1t", [128, D], F32)
    for k in range(8):
        ksl = slice(k * D, (k + 1) * D)
        r.v("tensor_scalar_mul", zt[:], wsb[:, ksl], brec[:])
        r.v("tensor_single_scalar", m1t[:], zt[:], 0.5, ALU.is_gt)
        r.v("tensor_single_scalar", zt[:], zt[:], -0.5, ALU.is_lt)
        r.v("tensor_tensor", wqsb[:, ksl], m1t[:], zt[:], ALU.subtract)

    gl_row = ph.sb("gl_row", [1, D], F32)
    bl_row = ph.sb("bl_row", [1, D], F32)
    r.dma(gl_row[:], ph.IN["ln_gamma"].rearrange("(o d) -> o d", o=1))
    r.dma(bl_row[:], ph.IN["ln_beta"].rearrange("(o d) -> o d", o=1))
    GLt = ph.sb("GLt", [128, D], F32)
    BLt = ph.sb("BLt", [128, D], F32)
    emit_row_bcast(ph, r, c, gl_row[:], GLt, D, ps_s)
    emit_row_bcast(ph, r, c, bl_row[:], BLt, D, ps_s)
    XNI = ph.dram("XNI", [T, D], BF16)
    xt = ph.sb("xt", [128, D], F32)
    xnit = ph.sb("xnit", [128, D], F32)
    gacc = ph.sb("gacc", [128, 1], F32)
    gtmp = ph.sb("gtmp", [128, 1], F32)
    r.g("memset", gacc[:], 0.0)
    scr = {"sq": ph.sb("sq", [128, D], F32),
           "s1": ph.sb("s1", [128, 1], F32),
           "s2": ph.sb("s2", [128, 1], F32),
           "s3": ph.sb("s3", [128, 1], F32),
           "xn": ph.sb("xnscr", [128, D], F32)}
    xnib = ph.sb("xnib", [128, D], BF16)
    for tc in range(T // 128):
        rows = slice(tc * 128, (tc + 1) * 128)
        r.dma(xt[:], XN1[rows, :])
        emit_ln_tile(ph, r, xt[:], xnit[:], scr, 1e-3, gb_tiles=(GLt, BLt))
        r.v("tensor_copy", xnib[:], xnit[:])
        r.dma(XNI[rows, :], xnib[:])
        r.v("reduce_max", gtmp[:], xnit[:], AX, True)
        r.v("tensor_tensor", gacc[:], gacc[:], gtmp[:], ALU.max)

    g1x1 = ph.sb("g1x1", [1, 1], F32)
    g1x128 = ph.sb("g1x128", [1, 128], F32)
    r.tr(ps_s[0:1, 0:128], gacc[:], c["id_f"][:])
    r.v("reduce_max", g1x1[:], ps_s[0:1, 0:128], AX)
    r.v("tensor_copy", g1x128[:], g1x1[0:1, 0:1].broadcast_to([1, 128]))
    CCIN = ph.dram("CCIN", [1, 128], F32)
    CCOUT = ph.dram("CCOUT", [1, 128], F32)
    r.dma(CCIN, g1x128[:], eng="gpsimd")
    r.comp("gpsimd", "collective_compute", "AllReduce", ALU.max,
           [[0, 1, 2, 3, 4, 5, 6, 7]], [CCIN], [CCOUT])
    gam11 = ph.sb("gam11", [1, 1], F32)
    r.dma(gam11[:], CCOUT[0:1, 0:1])
    gam128 = ph.sb("gam128", [128, 1], F32)
    r.mm(ps_s[0:128, 0:1], c["ones_r_f"][:], gam11[:])
    r.v("tensor_copy", gam128[:], ps_s[0:128, 0:1])
    sxq = ph.sb("sxq", [128, 1], F32)
    r.v("reciprocal", sxq[:], gam128[:])
    r.v("tensor_scalar_mul", sxq[:], sxq[:], 128.0)
    sy = ph.sb("sy", [128, 1], F32)
    r.v("tensor_tensor", sy[:], gam128[:], beta128[:], ALU.mult)
    r.v("tensor_scalar_mul", sy[:], sy[:], 1.0 / 128.0)

    CLIP = 128.0 - 1e-5
    xqf = [ph.sb(f"xqf{k}", [128, T], BF16) for k in range(8)]
    for k in range(8):
        r.dma(xqf[k][:], XNI[0:T, k * 128:(k + 1) * 128], transpose=True)
        r.v("tensor_scalar_mul", xqf[k][:], xqf[k][:], sxq[:])
        r.v("tensor_single_scalar", xqf[k][:], xqf[k][:], CLIP, ALU.min)
        r.v("tensor_single_scalar", xqf[k][:], xqf[k][:], -CLIP, ALU.max)
    psy = ph.ps("psy", [128, 512], F32)
    yt = ph.sb("yt", [128, D], F32)
    acct = ph.sb("acct", [128, D], F32)
    moet = ph.sb("moet", [128, D], F32)
    for tc in range(T // 128):
        rows = slice(tc * 128, (tc + 1) * 128)
        for n in range(2):
            nt = slice(n * 512, (n + 1) * 512)
            for k in range(8):
                r.mm(psy[:], xqf[k][:, tc * 128:(tc + 1) * 128],
                     wqsb[:, k * D + n * 512:k * D + (n + 1) * 512],
                     start=(k == 0), stop=(k == 7))
            r.v("tensor_scalar_mul", yt[:, nt], psy[:], sy[:])
        r.dma(acct[:], ACC[rows, :])
        r.dma(moet[:], MOE[rows, :])
        r.v("tensor_tensor", yt[:], yt[:], acct[:], ALU.add)
        r.v("tensor_tensor", yt[:], yt[:], moet[:], ALU.add)
        r.dma(OUT[rows, :], yt[:])


def kernel_unfused(**inputs):
    x = np.ascontiguousarray(inputs["L0_attn_x"], dtype=np.float32)
    xs = [np.ascontiguousarray(x[2 * i:2 * i + 2].reshape(T, D)) for i in range(NCORES)]
    inp = inputs

    ph = Ph("ln0")
    ph.inp("x", xs)
    ph.out("XN0T", [T, D], BF16)
    emit_ln0(ph)
    XN0T = [o["XN0T"] for o in ph.run()]

    ph = Ph("attn0")
    ph.inp("XN0T", XN0T)
    ph.inp("w_qkv", inp["L0_attn_w_qkv"])
    ph.inp("fc_code", inp["L0_attn_fc_code"])
    ph.inp("w_out", inp["L0_attn_w_out"])
    ph.out("AO", [T, D], F32)
    emit_attn0(ph)
    AO = [o["AO"] for o in ph.run()]

    ph = Ph("ffn0a")
    ph.inp("XN0T", XN0T)
    ph.inp("w1_w", inp["L0_ffn_w1_w"])
    ph.inp("w1_s", inp["L0_ffn_w1_s"])
    ph.inp("w2_w", inp["L0_ffn_w2_w"])
    ph.inp("w2_s", inp["L0_ffn_w2_s"])
    ph.out("H", [DH, T], BF16)
    emit_ffn0a(ph)
    H = [o["H"] for o in ph.run()]

    ph = Ph("ffn0b")
    ph.inp("H", H)
    ph.inp("h1_w", inp["L0_ffn_h1_w"])
    ph.inp("h1_s", inp["L0_ffn_h1_s"])
    ph.out("G1", [DH, T], BF16)
    emit_ffn0bc(ph, "h1_w", "h1_s", "H", "G1")
    G1 = [o["G1"] for o in ph.run()]
    del H

    ph = Ph("ffn0c")
    ph.inp("G1", G1)
    ph.inp("h2_w", inp["L0_ffn_h2_w"])
    ph.inp("h2_s", inp["L0_ffn_h2_s"])
    ph.out("G2", [DH, T], BF16)
    emit_ffn0bc(ph, "h2_w", "h2_s", "G1", "G2")
    G2 = [o["G2"] for o in ph.run()]
    del G1

    ph = Ph("ffn0d")
    ph.inp("G2", G2)
    ph.inp("w3_w", inp["L0_ffn_w3_w"])
    ph.inp("w3_s", inp["L0_ffn_w3_s"])
    ph.inp("x", xs)
    ph.inp("AO", AO)
    ph.out("X1", [T, D], F32)
    emit_ffn0d(ph)
    X1 = [o["X1"] for o in ph.run()]
    del G2, AO

    ph = Ph("l1pre")
    ph.inp("X1", X1)
    for nm in ("ln1_g", "ln1_b", "tm_mu_k", "tm_mu_v", "tm_mu_r",
               "Wk", "Wv", "Wr"):
        ph.inp(nm, inp["L1_attn_" + nm])
    ph.out("XN1", [T, D], F32)
    ph.out("E", [D, T], F32)
    ph.out("V1", [D, T], F32)
    ph.out("R", [D, T], F32)
    emit_l1pre(ph)
    o = ph.run()
    XN1 = [q["XN1"] for q in o]
    E = [q["E"] for q in o]
    V1 = [q["V1"] for q in o]
    R = [q["R"] for q in o]

    ph = Ph("scan")
    ph.inp("E", E)
    ph.inp("V1", V1)
    ph.inp("R", R)
    ph.inp("XN1", XN1)
    ph.inp("X1", X1)
    for nm in ("time_decay", "time_first", "Wo", "ln2_g", "ln2_b"):
        ph.inp(nm, inp["L1_attn_" + nm])
    ph.out("ACC", [T, D], F32)
    ph.out("EIT", [T, D], BF16)
    emit_scan(ph)
    o = ph.run()
    ACC = [q["ACC"] for q in o]
    EIT = [q["EIT"] for q in o]
    del E, V1, R, X1

    ph = Ph("router")
    ph.inp("EIT", EIT)
    ph.inp("router_w", inp["L1_attn_router_w"])
    ph.out("TW", [T, NE], F32)
    emit_router(ph)
    TW = [q["TW"] for q in ph.run()]

    ph = Ph("moe")
    ph.inp("EIT", EIT)
    ph.inp("TW", TW)
    for nm in ("cm_mu_k", "cm_mu_r", "eWk", "eWv", "eWr"):
        ph.inp(nm, inp["L1_attn_" + nm])
    ph.out("MOE", [T, D], F32)
    emit_moe(ph)
    MOE = [q["MOE"] for q in ph.run()]

    ph = Ph("ffn1")
    ph.inp("XN1", XN1)
    ph.inp("ACC", ACC)
    ph.inp("MOE", MOE)
    ph.inp("ln_gamma", inp["L1_ffn_ln_gamma"])
    ph.inp("ln_beta", inp["L1_ffn_ln_beta"])
    ph.inp("weight", inp["L1_ffn_weight"])
    ph.out("OUT", [T, D], F32)
    emit_ffn1(ph)
    OUT = [q["OUT"] for q in ph.run()]

    return np.concatenate([np.asarray(a, np.float32).reshape(2, S, D)
                           for a in OUT], axis=0)



_WMAP = {
    "w_qkv": "L0_attn_w_qkv", "fc_code": "L0_attn_fc_code",
    "w_out": "L0_attn_w_out",
    "w1_w": "L0_ffn_w1_w", "w1_s": "L0_ffn_w1_s",
    "w2_w": "L0_ffn_w2_w", "w2_s": "L0_ffn_w2_s",
    "h1_w": "L0_ffn_h1_w", "h1_s": "L0_ffn_h1_s",
    "h2_w": "L0_ffn_h2_w", "h2_s": "L0_ffn_h2_s",
    "w3_w": "L0_ffn_w3_w", "w3_s": "L0_ffn_w3_s",
    "ln1_g": "L1_attn_ln1_g", "ln1_b": "L1_attn_ln1_b",
    "ln2_g": "L1_attn_ln2_g", "ln2_b": "L1_attn_ln2_b",
    "tm_mu_k": "L1_attn_tm_mu_k", "tm_mu_v": "L1_attn_tm_mu_v",
    "tm_mu_r": "L1_attn_tm_mu_r",
    "time_decay": "L1_attn_time_decay", "time_first": "L1_attn_time_first",
    "Wk": "L1_attn_Wk", "Wv": "L1_attn_Wv", "Wr": "L1_attn_Wr",
    "Wo": "L1_attn_Wo",
    "router_w": "L1_attn_router_w",
    "cm_mu_k": "L1_attn_cm_mu_k", "cm_mu_r": "L1_attn_cm_mu_r",
    "eWk": "L1_attn_eWk", "eWv": "L1_attn_eWv", "eWr": "L1_attn_eWr",
    "weight": "L1_ffn_weight",
    "ln_gamma": "L1_ffn_ln_gamma", "ln_beta": "L1_ffn_ln_beta",
}


def kernel(**inputs):
    x = np.ascontiguousarray(inputs["L0_attn_x"], dtype=np.float32)
    xs = [np.ascontiguousarray(x[2 * i:2 * i + 2].reshape(T, D))
          for i in range(NCORES)]

    ph = Ph("mega")
    W = {"x": ph.inp("x", xs)}
    for short, long in _WMAP.items():
        W[short] = ph.inp(short, np.ascontiguousarray(inputs[long], dtype=np.float32))
    OUTap = ph.out("OUT", [T, D], F32)

    XN0T = ph.dram("XN0Td", [T, D], BF16)
    AOd = ph.dram("AOd", [T, D], F32)
    Hd = ph.dram("Hd", [DH, T], BF16)
    G1d = ph.dram("G1d", [DH, T], BF16)
    G2d = ph.dram("G2d", [DH, T], BF16)
    X1d = ph.dram("X1d", [T, D], F32)
    XN1d = ph.dram("XN1d", [T, D], F32)
    Ed = ph.dram("Ed", [D, T], F32)
    V1d = ph.dram("V1d", [D, T], F32)
    Rd = ph.dram("Rd", [D, T], F32)
    ACCd = ph.dram("ACCd", [T, D], F32)
    EITd = ph.dram("EITd", [T, D], BF16)
    TWd = ph.dram("TWd", [T, NE], F32)
    MOEd = ph.dram("MOEd", [T, D], F32)

    def sec(prefix, emit_fn, IN, OUTm, *args):
        ph.begin(prefix)
        ph.IN = IN
        ph.OUT = OUTm
        emit_fn(ph, *args)
        ph.end()

    sec("p0", emit_ln0, {"x": W["x"]}, {"XN0T": XN0T})
    sec("p1", emit_attn0,
        {"XN0T": XN0T, "w_qkv": W["w_qkv"], "fc_code": W["fc_code"],
         "w_out": W["w_out"]}, {"AO": AOd})
    sec("p2", emit_ffn0a,
        {"XN0T": XN0T, "w1_w": W["w1_w"], "w1_s": W["w1_s"],
         "w2_w": W["w2_w"], "w2_s": W["w2_s"]}, {"H": Hd})
    sec("p3", emit_ffn0bc,
        {"H": Hd, "h1_w": W["h1_w"], "h1_s": W["h1_s"]}, {"G1": G1d},
        "h1_w", "h1_s", "H", "G1")
    sec("p4", emit_ffn0bc,
        {"G1": G1d, "h2_w": W["h2_w"], "h2_s": W["h2_s"]}, {"G2": G2d},
        "h2_w", "h2_s", "G1", "G2")
    sec("p5", emit_ffn0d,
        {"G2": G2d, "w3_w": W["w3_w"], "w3_s": W["w3_s"], "x": W["x"],
         "AO": AOd}, {"X1": X1d})
    sec("p6", emit_l1pre,
        {"X1": X1d, "ln1_g": W["ln1_g"], "ln1_b": W["ln1_b"],
         "tm_mu_k": W["tm_mu_k"], "tm_mu_v": W["tm_mu_v"],
         "tm_mu_r": W["tm_mu_r"], "Wk": W["Wk"], "Wv": W["Wv"],
         "Wr": W["Wr"]},
        {"XN1": XN1d, "E": Ed, "V1": V1d, "R": Rd})
    sec("p7", emit_scan,
        {"E": Ed, "V1": V1d, "R": Rd, "XN1": XN1d, "X1": X1d,
         "time_decay": W["time_decay"], "time_first": W["time_first"],
         "Wo": W["Wo"], "ln2_g": W["ln2_g"], "ln2_b": W["ln2_b"]},
        {"ACC": ACCd, "EIT": EITd})
    sec("p8", emit_router, {"EIT": EITd, "router_w": W["router_w"]},
        {"TW": TWd})
    sec("p9", emit_moe,
        {"EIT": EITd, "TW": TWd, "cm_mu_k": W["cm_mu_k"],
         "cm_mu_r": W["cm_mu_r"], "eWk": W["eWk"], "eWv": W["eWv"],
         "eWr": W["eWr"]}, {"MOE": MOEd})
    sec("p10", emit_ffn1,
        {"XN1": XN1d, "ACC": ACCd, "MOE": MOEd, "ln_gamma": W["ln_gamma"],
         "ln_beta": W["ln_beta"], "weight": W["weight"]}, {"OUT": OUTap})

    OUT = [q["OUT"] for q in ph.run()]
    return np.concatenate([np.asarray(a, np.float32).reshape(2, S, D)
                           for a in OUT], axis=0)


# version 04
# speedup vs baseline: 1.4720x; 1.4720x over previous
import numpy as np
import ml_dtypes
from contextlib import ExitStack

import concourse.bass as bass
import concourse.mybir as mybir
from concourse.bass_utils import run_bass_kernel_spmd

F32 = mybir.dt.float32
BF16 = mybir.dt.bfloat16
AF = mybir.ActivationFunctionType
ALU = mybir.AluOpType
AX = mybir.AxisListType.X

NCORES = 8
NSEQ = 2
S = 2048
T = NSEQ * S
D = 1024
DH = 4096
NH = 16
HD = 64
CS = 16
NE = 8
FE = 2048

BF = ml_dtypes.bfloat16


def np2dt(dt):
    if dt == np.float32:
        return F32
    if dt == BF:
        return BF16
    raise ValueError(dt)


class Rec:

    ENGINES = ("sync", "scalar", "vector", "gpsimd", "tensor")

    def __init__(self, dsem, csem):
        self.dsem = dsem
        self.csem = csem
        self.streams = {e: [] for e in self.ENGINES}
        self.d = 0
        self.c = 0
        self.lw_d = {e: 0 for e in self.ENGINES}
        self.lw_co = {e: 0 for e in self.ENGINES}
        self.oc = {e: 0 for e in self.ENGINES}

    SKIP_OWN = {"tensor"}

    def _wait_all(self, eng):
        st = self.streams[eng]
        if self.d > self.lw_d[eng]:
            st.append(("wait_ge", (self.dsem, self.d * 16), {}, None))
            self.lw_d[eng] = self.d
        c_other = (self.c - self.oc[eng]) if eng in self.SKIP_OWN else self.c
        if c_other > self.lw_co[eng]:
            st.append(("wait_ge", (self.csem, self.c), {}, None))
            self.lw_co[eng] = c_other

    def dma(self, dst, src, eng="sync", **kw):
        self._wait_all(eng)
        self.streams[eng].append(("dma_start", (dst, src), kw, (self.dsem, 16)))
        self.d += 1

    def comp(self, eng, meth, *args, **kw):
        self._wait_all(eng)
        self.streams[eng].append((meth, args, kw, (self.csem, 1)))
        self.c += 1
        self.oc[eng] += 1

    def finish(self):
        self.streams["sync"].append(("wait_ge", (self.dsem, self.d * 16), {}, None))
        self.streams["sync"].append(("wait_ge", (self.csem, self.c), {}, None))

    def flush_into(self, block):
        for eng in self.ENGINES:
            stream = self.streams[eng]
            if not stream:
                continue

            def make(fn_stream):
                def _(engobj):
                    for meth, args, kw, inc in fn_stream:
                        ins = getattr(engobj, meth)(*args, **kw)
                        if inc is not None:
                            ins.then_inc(*inc)
                return _

            getattr(block, eng)(make(stream))
        self.streams = {e: [] for e in self.ENGINES}

    def mm(self, out, lhsT, rhs, start=True, stop=True):
        self.comp("tensor", "matmul", out, lhsT, rhs, start=start, stop=stop)

    def tr(self, out_psum, in_, ident):
        self.comp("tensor", "transpose", out_psum, in_, ident)

    def act(self, out, in_, func, **kw):
        self.comp("scalar", "activation", out, in_, func, **kw)

    def v(self, meth, *args, **kw):
        self.comp("vector", meth, *args, **kw)

    def g(self, meth, *args, **kw):
        self.comp("gpsimd", meth, *args, **kw)


class Ph:

    def __init__(self, name):
        self.nc = bass.Bass("TRN2", target_bir_lowering=False, debug=False)
        self.name = name
        self.es = ExitStack()
        self.dsem = self.es.enter_context(self.nc.semaphore("dsem"))
        self.csem = self.es.enter_context(self.nc.semaphore("csem"))
        self.r = Rec(self.dsem, self.csem)
        self.IN = {}
        self.OUT = {}
        self._in_arrays = {}
        self.prefix = ""
        self.sec_es = None

    def begin(self, prefix):
        self.prefix = prefix + "_"
        self.sec_es = ExitStack()

    def end(self):
        with self.nc.Block() as block:
            self.r.flush_into(block)
        self.sec_es.close()
        self.sec_es = None
        self.prefix = ""

    def inp(self, name, arrs):
        if isinstance(arrs, np.ndarray):
            arrs = [arrs] * NCORES
        a0 = arrs[0]
        ap = self.nc.dram_tensor(name, list(a0.shape), np2dt(a0.dtype),
                                 kind="ExternalInput").ap()
        self.IN[name] = ap
        self._in_arrays[name] = arrs
        return ap

    def out(self, name, shape, dtype):
        ap = self.nc.dram_tensor(name, list(shape), dtype,
                                 kind="ExternalOutput").ap()
        self.OUT[name] = ap
        return ap

    def dram(self, name, shape, dtype):
        return self.nc.dram_tensor(name, list(shape), dtype, kind="Internal").ap()

    def sb(self, name, shape, dtype):
        es = self.sec_es if self.sec_es is not None else self.es
        return es.enter_context(
            self.nc.sbuf_tensor(self.prefix + name, list(shape), dtype))

    def ps(self, name, shape, dtype=F32):
        es = self.sec_es if self.sec_es is not None else self.es
        return es.enter_context(
            self.nc.psum_tensor(self.prefix + name, list(shape), dtype))

    def run(self, trace=False):
        self.r.finish()
        with self.nc.Block() as block:
            self.r.flush_into(block)
        self.es.close()
        in_maps = [{k: self._in_arrays[k][i] for k in self._in_arrays}
                   for i in range(NCORES)]
        res = run_bass_kernel_spmd(self.nc, in_maps,
                                   core_ids=list(range(NCORES)), trace=trace)
        return res.results



def emit_const_tiles(ph):
    r = ph.r
    c = {}
    c["id_bf"] = ph.sb("id_bf", [128, 128], BF16)
    c["id_f"] = ph.sb("id_f", [128, 128], F32)
    c["tri"] = ph.sb("tri", [128, 128], F32)
    c["ones_r_bf"] = ph.sb("ones_r_bf", [1, 128], BF16)
    c["ones_r_f"] = ph.sb("ones_r_f", [1, 128], F32)
    c["ones_c_bf"] = ph.sb("ones_c_bf", [128, 1], BF16)
    c["ones_c_f"] = ph.sb("ones_c_f", [128, 1], F32)
    for t_ in ("id_bf", "id_f"):
        r.g("memset", c[t_][:], 0.0)
        r.g("affine_select", out=c[t_][:], in_=c[t_][:],
            compare_op=ALU.not_equal, fill=1.0, base=0,
            pattern=[[-1, 128]], channel_multiplier=1)
    r.g("memset", c["tri"][:], 0.0)
    r.g("affine_select", out=c["tri"][:], in_=c["tri"][:],
        compare_op=ALU.is_gt, fill=1.0, base=0,
        pattern=[[-1, 128]], channel_multiplier=1)
    for t_ in ("ones_r_bf", "ones_r_f", "ones_c_bf", "ones_c_f"):
        r.g("memset", c[t_][:], 1.0)
    return c


def emit_ln_tile(ph, r, x_tile, out_ap, scr, eps, gb_tiles=None,
                 negmean_out=None, rstd_out=None):
    sq, s1, s2, s3 = scr["sq"], scr["s1"], scr["s2"], scr["s3"]
    r.v("tensor_reduce", s1[:], x_tile, AX, ALU.add)
    r.v("tensor_scalar_mul", s1[:], s1[:], -1.0 / D)
    r.act(sq[:], x_tile, AF.Square)
    r.v("tensor_reduce", s2[:], sq[:], AX, ALU.add)
    r.v("tensor_scalar_mul", s2[:], s2[:], 1.0 / D)
    r.v("tensor_mul", s3[:], s1[:], s1[:])
    r.v("tensor_sub", s2[:], s2[:], s3[:])
    r.v("tensor_single_scalar", s2[:], s2[:], eps, ALU.add)
    r.act(s2[:], s2[:], AF.Ln)
    r.v("tensor_scalar_mul", s2[:], s2[:], -0.5)
    r.act(s2[:], s2[:], AF.Exp)
    if negmean_out is not None:
        r.v("tensor_copy", negmean_out, s1[:])
    if rstd_out is not None:
        r.v("tensor_copy", rstd_out, s2[:])
    if gb_tiles is None:
        r.v("tensor_scalar", out_ap, x_tile, s1[:], s2[:], ALU.add, ALU.mult)
    else:
        G, B = gb_tiles
        r.v("tensor_scalar", scr["xn"][:], x_tile, s1[:], s2[:], ALU.add, ALU.mult)
        r.v("tensor_mul", scr["xn"][:], scr["xn"][:], G[:])
        r.v("tensor_tensor", out_ap, scr["xn"][:], B[:], ALU.add)


def emit_row_bcast(ph, r, c, row_f32_ap, out_tile, n, pst):
    for off in range(0, n, 512):
        w = min(512, n - off)
        r.mm(pst[:, 0:w], c["ones_r_f"][:], row_f32_ap[0:1, off:off + w])
        r.v("tensor_copy", out_tile[:, off:off + w], pst[:, 0:w])



def emit_ln0(ph):
    r = ph.r
    x = ph.IN["x"]
    xn0t = ph.OUT["XN0T"]
    xt = ph.sb("xt", [128, D], F32)
    xnb = ph.sb("xnb", [128, D], BF16)
    scr = {"sq": ph.sb("sq", [128, D], F32),
           "s1": ph.sb("s1", [128, 1], F32),
           "s2": ph.sb("s2", [128, 1], F32),
           "s3": ph.sb("s3", [128, 1], F32)}
    for tc in range(T // 128):
        r.dma(xt[:], x[tc * 128:(tc + 1) * 128, :])
        emit_ln_tile(ph, r, xt[:], xnb[:], scr, 1e-5)
        r.dma(xn0t[tc * 128:(tc + 1) * 128, :], xnb[:])



def emit_attn0(ph):
    r = ph.r
    XN0T = ph.IN["XN0T"]
    w_qkv = ph.IN["w_qkv"]
    fc_code = ph.IN["fc_code"].rearrange("o h c d -> (o h c) d")
    w_out = ph.IN["w_out"]
    AO = ph.OUT["AO"]

    c = emit_const_tiles(ph)
    tri_bf = ph.sb("tri_bf", [128, 128], BF16)
    r.v("tensor_copy", tri_bf[:], c["tri"][:])

    wqkv_bf = ph.dram("wqkv_bf", [3 * D, D], BF16)
    WQKVT = ph.dram("WQKVT", [D, 3 * D], BF16)
    wout_bf = ph.dram("wout_bf", [D, D], BF16)
    WOUTT = ph.dram("WOUTT", [D, D], BF16)
    QKV = ph.dram("QKV", [3 * D, T], BF16)
    r.dma(wqkv_bf, w_qkv, eng="gpsimd")
    r.dma(wout_bf, w_out, eng="gpsimd")
    tbig = ph.sb("tbig", [128, 3 * D], BF16)
    for k in range(8):
        r.dma(tbig[:], wqkv_bf[0:3 * D, k * 128:(k + 1) * 128], transpose=True)
        r.dma(WQKVT[k * 128:(k + 1) * 128, :], tbig[:])
    for k in range(8):
        r.dma(tbig[:, 0:D], wout_bf[0:D, k * 128:(k + 1) * 128], transpose=True)
        r.dma(WOUTT[k * 128:(k + 1) * 128, :], tbig[:, 0:D])

    ps_tr = ph.ps("ps_tr", [128, 1024], BF16)
    ps_a = ph.ps("ps_a", [128, 512], F32)
    ps_qc = ph.ps("ps_qc", [128, 512], F32)
    ps_cum = ph.ps("ps_cum", [128, 512], F32)
    ps_xo = ph.ps("ps_xo", [128, 512], F32)
    ps_M = ph.ps("ps_M", [128, 512], F32)
    ps_car = ph.ps("ps_car", [128, 512], F32)

    code_nat = ph.sb("code_nat", [16, 64], BF16)
    codeT = ph.sb("codeT", [64, 16 * NH], BF16)
    for h in range(NH):
        r.dma(code_nat[:], fc_code[h * 16:(h + 1) * 16, :], eng="gpsimd")
        r.tr(ps_tr[0:64, 0:16], code_nat[:], c["id_bf"][0:16, 0:16])
        r.v("tensor_scalar_mul", codeT[:, h * 16:(h + 1) * 16],
            ps_tr[0:64, 0:16], 0.125)

    xnf = [ph.sb(f"xnf{k}", [128, S], BF16) for k in range(8)]
    wm = ph.sb("wm", [128, 8 * 128], BF16)
    qkv_t = ph.sb("qkv_t", [128, S], BF16)
    attn_fm = [ph.sb(f"afm{k}", [128, S], BF16) for k in range(8)]
    woutt_sb = ph.sb("woutt", [128, 8 * D], BF16)
    ao_t = ph.sb("ao_t", [128, D], F32)
    for k in range(8):
        r.dma(woutt_sb[:, k * D:(k + 1) * D], WOUTT[k * 128:(k + 1) * 128, :])

    qf = ph.sb("qf", [64, S], BF16)
    kf = ph.sb("kf", [64, S], BF16)
    vf = ph.sb("vf", [64, S], BF16)
    kcm = ph.sb("kcm", [16, S], F32)
    kexp_bf = ph.sb("kexp_bf", [16, S], BF16)
    nmax = ph.sb("nmax", [16, 1], F32)
    S_tm = ph.sb("S_tm", [128, 16], BF16)
    V_tm = ph.sb("V_tm", [128, 64], BF16)
    qc = ph.sb("qc", [128, 16], F32)
    q2b = ph.sb("q2b", [128, 16], BF16)
    q2T = ph.sb("q2T", [16, 128], BF16)
    Am = ph.sb("Am", [128, 128], BF16)
    kc_sb = ph.sb("kc_sb", [128, 16], F32)
    dden = ph.sb("dden", [128, 16], F32)
    M_sb = ph.sb("M_sb", [16, 64], BF16)
    carry_bf = ph.sb("carry_bf", [1, 16], BF16)
    smax = ph.sb("smax", [128, 1], F32)
    ssum = ph.sb("ssum", [128, 1], F32)
    srec = ph.sb("srec", [128, 1], F32)

    for s in range(NSEQ):
        srow = s * S
        for k in range(8):
            r.dma(xnf[k][:], XN0T[srow:srow + S, k * 128:(k + 1) * 128],
                  transpose=True)
        for m in range(24):
            r.dma(wm[:].rearrange("p (k o) -> p k o", k=8),
                  WQKVT[:, m * 128:(m + 1) * 128].rearrange("(k p) o -> p k o", p=128))
            for n in range(4):
                for k in range(8):
                    r.mm(ps_a[:, 0:512], wm[:, k * 128:(k + 1) * 128],
                         xnf[k][:, n * 512:(n + 1) * 512],
                         start=(k == 0), stop=(k == 7))
                r.v("tensor_copy", qkv_t[:, n * 512:(n + 1) * 512], ps_a[:, 0:512])
            r.dma(QKV[m * 128:(m + 1) * 128, srow:srow + S], qkv_t[:])

        for h in range(NH):
            r.dma(qf[:], QKV[h * HD:(h + 1) * HD, srow:srow + S])
            r.dma(kf[:], QKV[D + h * HD:D + (h + 1) * HD, srow:srow + S])
            r.dma(vf[:], QKV[2 * D + h * HD:2 * D + (h + 1) * HD, srow:srow + S])
            ct = codeT[:, h * 16:(h + 1) * 16]
            for n in range(4):
                r.mm(ps_a[0:16, 0:512], ct, kf[:, n * 512:(n + 1) * 512])
                r.v("tensor_copy", kcm[:, n * 512:(n + 1) * 512], ps_a[0:16, 0:512])
            r.v("reduce_max", nmax[:], kcm[:], AX)
            r.v("tensor_scalar_mul", nmax[:], nmax[:], -1.0)
            r.act(kcm[:], kcm[:], AF.Exp, bias=nmax[:])
            r.v("tensor_copy", kexp_bf[:], kcm[:])
            for j in range(S // 128):
                tok = slice(j * 128, (j + 1) * 128)
                last = (j == S // 128 - 1)
                if j == 0:
                    r.g("memset", M_sb[:], 0.0)
                    r.g("memset", carry_bf[:], 0.0)
                else:
                    r.v("tensor_copy", M_sb[:], ps_M[0:16, 0:64])
                    r.v("tensor_copy", carry_bf[:], ps_car[0:1, 0:16])
                r.tr(ps_tr[0:128, 0:16], kexp_bf[:, tok], c["id_bf"][0:16, 0:16])
                r.v("tensor_copy", S_tm[:], ps_tr[0:128, 0:16])
                r.tr(ps_tr[0:128, 128:192], vf[:, tok], c["id_bf"][0:64, 0:64])
                r.v("tensor_copy", V_tm[:], ps_tr[0:128, 128:192])
                r.mm(ps_cum[0:128, 0:16], tri_bf[:], S_tm[:], start=True, stop=False)
                r.mm(ps_cum[0:128, 0:16], c["ones_r_bf"][:], carry_bf[:],
                     start=False, stop=True)
                r.v("tensor_single_scalar", kc_sb[:], ps_cum[0:128, 0:16], 1e-9, ALU.add)
                r.v("reciprocal", dden[:], kc_sb[:])
                r.mm(ps_qc[0:128, 0:16], qf[:, tok], ct)
                r.v("reduce_max", smax[:], ps_qc[0:128, 0:16], AX)
                r.v("tensor_scalar_mul", smax[:], smax[:], -1.0)
                r.act(qc[:], ps_qc[0:128, 0:16], AF.Exp, bias=smax[:])
                r.v("tensor_reduce", ssum[:], qc[:], AX, ALU.add)
                r.v("reciprocal", srec[:], ssum[:])
                r.v("tensor_scalar", qc[:], qc[:], srec[:], 0.125, ALU.mult, ALU.mult)
                r.v("tensor_tensor", q2b[:], qc[:], dden[:], ALU.mult)
                r.tr(ps_tr[0:16, 256:384], q2b[:], c["id_bf"][:])
                r.v("tensor_copy", q2T[:], ps_tr[0:16, 256:384])
                r.mm(ps_a[0:128, 0:128], kexp_bf[:, tok], q2T[:])
                r.v("tensor_tensor", Am[:], ps_a[0:128, 0:128], c["tri"][:], ALU.mult)
                r.mm(ps_xo[0:64, 0:128], M_sb[:], q2T[:], start=True, stop=False)
                r.mm(ps_xo[0:64, 0:128], V_tm[:], Am[:], start=False, stop=True)
                poff = 64 * (h % 2)
                r.v("tensor_copy", attn_fm[h // 2][poff:poff + 64, tok],
                    ps_xo[0:64, 0:128])
                r.mm(ps_M[0:16, 0:64], S_tm[:], V_tm[:], start=(j == 0), stop=last)
                r.mm(ps_car[0:1, 0:16], c["ones_c_bf"][:], S_tm[:], start=(j == 0), stop=last)

        for tc in range(S // 128):
            for n in range(2):
                for k in range(8):
                    r.mm(ps_a[:, 0:512], attn_fm[k][:, tc * 128:(tc + 1) * 128],
                         woutt_sb[:, k * D + n * 512:k * D + (n + 1) * 512],
                         start=(k == 0), stop=(k == 7))
                r.v("tensor_copy", ao_t[:, n * 512:(n + 1) * 512], ps_a[:, 0:512])
            r.dma(AO[srow + tc * 128:srow + (tc + 1) * 128, :], ao_t[:])



def emit_wprep_T(ph, r, src_ap, rows, cols, name):
    wbf = ph.dram(name + "_bf", [rows, cols], BF16)
    wT = ph.dram(name + "_T", [cols, rows], BF16)
    r.dma(wbf, src_ap, eng="gpsimd")
    tb = ph.sb(name + "_tb", [128, rows], BF16)
    for k in range(cols // 128):
        r.dma(tb[:], wbf[0:rows, k * 128:(k + 1) * 128], transpose=True)
        r.dma(wT[k * 128:(k + 1) * 128, :], tb[:])
    return wT


def emit_scale_cols(ph, r, c, s_ap, n, name, pst):
    nm = n // 128
    tmp = ph.sb(name + "_ld", [nm, 128], F32)
    out = ph.sb(name + "_c", [128, nm], F32)
    r.dma(tmp[:], s_ap.rearrange("(m p) -> m p", p=128))
    r.tr(pst[0:128, 0:nm], tmp[:], c["id_f"][0:nm, 0:nm])
    r.v("tensor_copy", out[:], pst[0:128, 0:nm])
    return out


def emit_ffn0a(ph):
    r = ph.r
    XN0T = ph.IN["XN0T"]
    H = ph.OUT["H"]
    c = emit_const_tiles(ph)
    pscr = ph.ps("pscr", [128, 512], F32)
    W1T = emit_wprep_T(ph, r, ph.IN["w1_w"], DH, D, "w1")
    W2T = emit_wprep_T(ph, r, ph.IN["w2_w"], DH, D, "w2")
    s1c = emit_scale_cols(ph, r, c, ph.IN["w1_s"], DH, "s1", pscr)
    s2c = emit_scale_cols(ph, r, c, ph.IN["w2_s"], DH, "s2", pscr)

    xnf = [ph.sb(f"xnf{k}", [128, T], BF16) for k in range(8)]
    for k in range(8):
        r.dma(xnf[k][:], XN0T[0:T, k * 128:(k + 1) * 128], transpose=True)
    w1m = ph.sb("w1m", [128, 8 * 128], BF16)
    w2m = ph.sb("w2m", [128, 8 * 128], BF16)
    tsil = ph.sb("tsil", [128, 512], F32)
    tmul = ph.sb("tmul", [128, 512], F32)
    h_row = ph.sb("h_row", [128, T], BF16)
    ps1 = ph.ps("ps1", [128, 512], F32)
    ps2 = ph.ps("ps2", [128, 512], F32)
    for m in range(DH // 128):
        r.dma(w1m[:].rearrange("p (k o) -> p k o", k=8),
              W1T[:, m * 128:(m + 1) * 128].rearrange("(k p) o -> p k o", p=128))
        r.dma(w2m[:].rearrange("p (k o) -> p k o", k=8),
              W2T[:, m * 128:(m + 1) * 128].rearrange("(k p) o -> p k o", p=128))
        for n in range(T // 512):
            nt = slice(n * 512, (n + 1) * 512)
            for k in range(8):
                r.mm(ps1[:], w1m[:, k * 128:(k + 1) * 128], xnf[k][:, nt],
                     start=(k == 0), stop=(k == 7))
            for k in range(8):
                r.mm(ps2[:], w2m[:, k * 128:(k + 1) * 128], xnf[k][:, nt],
                     start=(k == 0), stop=(k == 7))
            r.act(tsil[:], ps1[:], AF.Silu, scale=s1c[:, m:m + 1])
            r.v("tensor_scalar_mul", tmul[:], ps2[:], s2c[:, m:m + 1])
            r.v("tensor_tensor", h_row[:, nt], tsil[:], tmul[:], ALU.mult)
        r.dma(H[m * 128:(m + 1) * 128, :], h_row[:])


def emit_ffn0bc(ph, win, wsn, inname, outname):
    r = ph.r
    HI = ph.IN[inname]
    GO = ph.OUT[outname]
    c = emit_const_tiles(ph)
    pscr = ph.ps("pscr", [128, 512], F32)
    WT = emit_wprep_T(ph, r, ph.IN[win], DH, DH, win)
    sc = emit_scale_cols(ph, r, c, ph.IN[wsn], DH, wsn, pscr)

    PAN = 2048
    hp = [ph.sb(f"hp{k}", [128, PAN], BF16) for k in range(DH // 128)]
    wtm = ph.sb("wtm", [128, (DH // 128) * 128], BF16)
    g_row = ph.sb("g_row", [128, PAN], BF16)
    psg = ph.ps("psg", [128, 512], F32)
    nk = DH // 128
    for p in range(T // PAN):
        pc = slice(p * PAN, (p + 1) * PAN)
        for k in range(nk):
            r.dma(hp[k][:], HI[k * 128:(k + 1) * 128, pc])
        for m in range(nk):
            r.dma(wtm[:].rearrange("p (k o) -> p k o", k=nk),
                  WT[:, m * 128:(m + 1) * 128].rearrange("(k p) o -> p k o", p=128))
            for n in range(PAN // 512):
                nt = slice(n * 512, (n + 1) * 512)
                for k in range(nk):
                    r.mm(psg[:], wtm[:, k * 128:(k + 1) * 128], hp[k][:, nt],
                         start=(k == 0), stop=(k == nk - 1))
                r.act(g_row[:, nt], psg[:], AF.Gelu, scale=sc[:, m:m + 1])
            r.dma(GO[m * 128:(m + 1) * 128, pc], g_row[:])


def emit_ffn0d(ph):
    r = ph.r
    G2 = ph.IN["G2"]
    x = ph.IN["x"]
    AO = ph.IN["AO"]
    X1 = ph.OUT["X1"]
    c = emit_const_tiles(ph)
    pscr = ph.ps("pscr", [128, 512], F32)
    W3T = emit_wprep_T(ph, r, ph.IN["w3_w"], D, DH, "w3")

    w3sb = ph.sb("w3sb", [128, (DH // 128) * D], BF16)
    for k in range(DH // 128):
        r.dma(w3sb[:, k * D:(k + 1) * D], W3T[k * 128:(k + 1) * 128, :])
    s3row = ph.sb("s3row", [1, D], F32)
    r.dma(s3row[:], ph.IN["w3_s"].rearrange("(o d) -> o d", o=1))
    W3S = ph.sb("W3S", [128, D], F32)
    emit_row_bcast(ph, r, c, s3row[:], W3S, D, pscr)

    g2m = ph.sb("g2m", [128, (DH // 128) * 128], BF16)
    yt = ph.sb("yt", [128, D], F32)
    xt = ph.sb("xt", [128, D], F32)
    aot = ph.sb("aot", [128, D], F32)
    psd = ph.ps("psd", [128, 512], F32)
    nk = DH // 128
    for tc in range(T // 128):
        r.dma(g2m[:].rearrange("p (k t) -> p k t", k=nk),
              G2[:, tc * 128:(tc + 1) * 128].rearrange("(k p) t -> p k t", p=128))
        r.dma(xt[:], x[tc * 128:(tc + 1) * 128, :])
        r.dma(aot[:], AO[tc * 128:(tc + 1) * 128, :])
        for n in range(2):
            nt = slice(n * 512, (n + 1) * 512)
            for k in range(nk):
                r.mm(psd[:], g2m[:, k * 128:(k + 1) * 128],
                     w3sb[:, k * D + n * 512:k * D + (n + 1) * 512],
                     start=(k == 0), stop=(k == nk - 1))
            r.v("tensor_tensor", yt[:, nt], psd[:], W3S[:, nt], ALU.mult)
        r.v("tensor_tensor", yt[:], yt[:], xt[:], ALU.add)
        r.v("tensor_tensor", yt[:], yt[:], aot[:], ALU.add)
        r.dma(X1[tc * 128:(tc + 1) * 128, :], yt[:])



def emit_l1pre(ph):
    r = ph.r
    X1 = ph.IN["X1"]
    XN1 = ph.OUT["XN1"]
    Ef = ph.OUT["E"]
    V1f = ph.OUT["V1"]
    Rf = ph.OUT["R"]
    c = emit_const_tiles(ph)

    g_row = ph.sb("g_row", [1, D], F32)
    b_row = ph.sb("b_row", [1, D], F32)
    r.dma(g_row[:], ph.IN["ln1_g"].rearrange("(o d) -> o d", o=1))
    r.dma(b_row[:], ph.IN["ln1_b"].rearrange("(o d) -> o d", o=1))
    pscr = ph.ps("pscr", [128, 512], F32)
    G1t = ph.sb("G1t", [128, D], F32)
    B1t = ph.sb("B1t", [128, D], F32)
    emit_row_bcast(ph, r, c, g_row[:], G1t, D, pscr)
    emit_row_bcast(ph, r, c, b_row[:], B1t, D, pscr)

    A_lnT = ph.dram("A_lnT", [T, D], BF16)
    xt = ph.sb("xt", [128, D], F32)
    xn1t = ph.sb("xn1t", [128, D], F32)
    abf = ph.sb("abf", [128, D], BF16)
    scr = {"sq": ph.sb("sq", [128, D], F32),
           "s1": ph.sb("s1", [128, 1], F32),
           "s2": ph.sb("s2", [128, 1], F32),
           "s3": ph.sb("s3", [128, 1], F32),
           "xn": ph.sb("xnscr", [128, D], F32)}
    for tc in range(T // 128):
        rows = slice(tc * 128, (tc + 1) * 128)
        r.dma(xt[:], X1[rows, :])
        emit_ln_tile(ph, r, xt[:], xn1t[:], scr, 1e-5)
        r.dma(XN1[rows, :], xn1t[:])
        emit_ln_tile(ph, r, xn1t[:], abf[:], scr, 1e-5, gb_tiles=(G1t, B1t))
        r.dma(A_lnT[rows, :], abf[:])

    muk = emit_scale_cols(ph, r, c, ph.IN["tm_mu_k"], D, "muk", pscr)
    muv = emit_scale_cols(ph, r, c, ph.IN["tm_mu_v"], D, "muv", pscr)
    mur = emit_scale_cols(ph, r, c, ph.IN["tm_mu_r"], D, "mur", pscr)

    wk_bf = ph.dram("wk_bf", [D, D], BF16)
    wv_bf = ph.dram("wv_bf", [D, D], BF16)
    wr_bf = ph.dram("wr_bf", [D, D], BF16)
    r.dma(wk_bf, ph.IN["Wk"], eng="gpsimd")
    r.dma(wv_bf, ph.IN["Wv"], eng="gpsimd")
    r.dma(wr_bf, ph.IN["Wr"], eng="gpsimd")

    xsf = [ph.sb(f"xsf{k}", [128, T], BF16) for k in range(8)]
    abuf = ph.sb("abuf", [128, S + 16], BF16)
    dbuf = ph.sb("dbuf", [128, S], BF16)
    wsb = ph.sb("wsb", [128, 8 * D], BF16)
    erow = ph.sb("erow", [128, T], F32)
    psp = ph.ps("psp", [128, 512], F32)
    r.g("memset", abuf[:, 15:16], 0.0)

    for st, (mu, wbf, outdram) in enumerate(
            [(muk, wk_bf, Ef), (muv, wv_bf, V1f), (mur, wr_bf, Rf)]):
        for cc in range(8):
            for s in range(NSEQ):
                r.dma(abuf[:, 16:16 + S],
                      A_lnT[s * S:(s + 1) * S, cc * 128:(cc + 1) * 128],
                      transpose=True)
                a_sl = abuf[:, 16:16 + S]
                xx_sl = abuf[:, 15:15 + S]
                r.v("tensor_tensor", dbuf[:], a_sl, xx_sl, ALU.subtract)
                r.v("scalar_tensor_tensor", xsf[cc][:, s * S:(s + 1) * S],
                    dbuf[:], mu[:, cc:cc + 1], xx_sl, ALU.mult, ALU.add)
        for k in range(8):
            r.dma(wsb[:, k * D:(k + 1) * D], wbf[k * 128:(k + 1) * 128, :])
        for m in range(8):
            for n in range(T // 512):
                nt = slice(n * 512, (n + 1) * 512)
                for k in range(8):
                    r.mm(psp[:], wsb[:, k * D + m * 128:k * D + (m + 1) * 128],
                         xsf[k][:, nt], start=(k == 0), stop=(k == 7))
                if st == 0:
                    r.act(erow[:, nt], psp[:], AF.Exp)
                elif st == 1:
                    r.v("tensor_copy", erow[:, nt], psp[:])
                else:
                    r.act(erow[:, nt], psp[:], AF.Sigmoid)
            r.dma(outdram[m * 128:(m + 1) * 128, :], erow[:])


def emit_scan(ph):
    r = ph.r
    Ef = ph.IN["E"]
    V1f = ph.IN["V1"]
    Rf = ph.IN["R"]
    XN1 = ph.IN["XN1"]
    X1 = ph.IN["X1"]
    ACC = ph.OUT["ACC"]
    EIT = ph.OUT["EIT"]
    c = emit_const_tiles(ph)

    pscr = ph.ps("pscr", [128, 512], F32)
    tdc = emit_scale_cols(ph, r, c, ph.IN["time_decay"], D, "tdc", pscr)
    tfc = emit_scale_cols(ph, r, c, ph.IN["time_first"], D, "tfc", pscr)
    lam = ph.sb("lam", [128, 8], F32)
    eu = ph.sb("eu", [128, 8], F32)
    r.act(lam[:], tdc[:], AF.Exp)
    r.v("tensor_scalar_mul", lam[:], lam[:], -1.0)
    r.act(lam[:], lam[:], AF.Exp)
    r.act(eu[:], tfc[:], AF.Exp)

    wo_bf = ph.dram("wo_bf", [D, D], BF16)
    r.dma(wo_bf, ph.IN["Wo"], eng="gpsimd")

    ec = ph.sb("ec", [128, S], F32)
    vc = ph.sb("vc", [128, S], F32)
    rc = ph.sb("rc", [128, S], F32)
    za = ph.sb("za", [128, S], F32)
    Ab = ph.sb("Ab", [128, S + 16], F32)
    Bb = ph.sb("Bb", [128, S + 16], F32)
    ek = ph.sb("ek", [128, S], F32)
    nb = ph.sb("nb", [128, S], F32)
    db = ph.sb("db", [128, S], F32)
    RW = [ph.sb(f"RW{k}", [128, T], BF16) for k in range(8)]
    r.g("memset", Ab[:, 15:16], 0.0)
    r.g("memset", Bb[:, 15:16], 0.0)

    for cc in range(8):
        rows = slice(cc * 128, (cc + 1) * 128)
        for s in range(NSEQ):
            cols = slice(s * S, (s + 1) * S)
            r.dma(ec[:], Ef[rows, cols])
            r.dma(vc[:], V1f[rows, cols])
            r.dma(rc[:], Rf[rows, cols])
            lamb = lam[:, cc:cc + 1].broadcast_to([128, S])
            r.v("tensor_tensor", za[:], ec[:], vc[:], ALU.mult)
            r.v("tensor_tensor_scan", Ab[:, 16:16 + S], lamb, za[:], 0.0,
                ALU.mult, ALU.add)
            r.v("tensor_tensor_scan", Bb[:, 16:16 + S], lamb, ec[:], 0.0,
                ALU.mult, ALU.add)
            r.v("tensor_scalar_mul", ek[:], ec[:], eu[:, cc:cc + 1])
            r.v("tensor_tensor", nb[:], ek[:], vc[:], ALU.mult)
            r.v("tensor_tensor", nb[:], nb[:], Ab[:, 15:15 + S], ALU.add)
            r.v("tensor_tensor", db[:], ek[:], Bb[:, 15:15 + S], ALU.add)
            r.v("reciprocal", db[:], db[:])
            r.v("tensor_tensor", nb[:], nb[:], db[:], ALU.mult)
            r.v("tensor_tensor", RW[cc][:, cols], nb[:], rc[:], ALU.mult)

    wosb = ph.sb("wosb", [128, 8 * D], BF16)
    for k in range(8):
        r.dma(wosb[:, k * D:(k + 1) * D], wo_bf[k * 128:(k + 1) * 128, :])
    g2_row = ph.sb("g2_row", [1, D], F32)
    b2_row = ph.sb("b2_row", [1, D], F32)
    r.dma(g2_row[:], ph.IN["ln2_g"].rearrange("(o d) -> o d", o=1))
    r.dma(b2_row[:], ph.IN["ln2_b"].rearrange("(o d) -> o d", o=1))
    G2t = ph.sb("G2t", [128, D], F32)
    B2t = ph.sb("B2t", [128, D], F32)
    emit_row_bcast(ph, r, c, g2_row[:], G2t, D, pscr)
    emit_row_bcast(ph, r, c, b2_row[:], B2t, D, pscr)

    psw = ph.ps("psw", [128, 512], F32)
    yt = ph.sb("yt", [128, D], F32)
    xnt = ph.sb("xnt", [128, D], F32)
    x1t = ph.sb("x1t", [128, D], F32)
    eib = ph.sb("eib", [128, D], BF16)
    scr = {"sq": ph.sb("sq", [128, D], F32),
           "s1": ph.sb("s1", [128, 1], F32),
           "s2": ph.sb("s2", [128, 1], F32),
           "s3": ph.sb("s3", [128, 1], F32),
           "xn": ph.sb("xnscr", [128, D], F32)}
    for tcg in range(T // 128):
        rows = slice(tcg * 128, (tcg + 1) * 128)
        s = tcg // (S // 128)
        tc = tcg % (S // 128)
        for n in range(2):
            nt = slice(n * 512, (n + 1) * 512)
            for k in range(8):
                r.mm(psw[:], RW[k][:, tcg * 128:(tcg + 1) * 128],
                     wosb[:, k * D + n * 512:k * D + (n + 1) * 512],
                     start=(k == 0), stop=(k == 7))
            r.v("tensor_copy", yt[:, nt], psw[:])
        r.dma(xnt[:], XN1[rows, :])
        r.dma(x1t[:], X1[rows, :])
        r.v("tensor_tensor", yt[:], yt[:], xnt[:], ALU.add)
        emit_ln_tile(ph, r, yt[:], eib[:], scr, 1e-5, gb_tiles=(G2t, B2t))
        r.dma(EIT[rows, :], eib[:])
        r.v("tensor_tensor", yt[:], yt[:], x1t[:], ALU.add)
        r.dma(ACC[rows, :], yt[:])


def emit_router(ph):
    r = ph.r
    EIT = ph.IN["EIT"]
    TW = ph.OUT["TW"]
    rw_bf = ph.dram("rw_bf", [D, NE], BF16)
    r.dma(rw_bf, ph.IN["router_w"], eng="gpsimd")
    rwsb = ph.sb("rwsb", [128, 8 * NE], BF16)
    for k in range(8):
        r.dma(rwsb[:, k * NE:(k + 1) * NE], rw_bf[k * 128:(k + 1) * 128, :])
    eif = [ph.sb(f"eif{k}", [128, T], BF16) for k in range(8)]
    for k in range(8):
        r.dma(eif[k][:], EIT[0:T, k * 128:(k + 1) * 128], transpose=True)
    psr = ph.ps("psr", [128, 512], F32)
    pt = ph.sb("pt", [128, NE], F32)
    p2 = ph.sb("p2", [128, NE], F32)
    twt = ph.sb("twt", [128, NE], F32)
    nm = ph.sb("nm", [128, 1], F32)
    sm = ph.sb("sm", [128, 1], F32)
    v1 = ph.sb("v1", [128, 1], F32)
    v2 = ph.sb("v2", [128, 1], F32)
    for tc in range(T // 128):
        for k in range(8):
            r.mm(psr[:, 0:NE], eif[k][:, tc * 128:(tc + 1) * 128],
                 rwsb[:, k * NE:(k + 1) * NE], start=(k == 0), stop=(k == 7))
        r.v("reduce_max", nm[:], psr[:, 0:NE], AX)
        r.v("tensor_scalar_mul", nm[:], nm[:], -1.0)
        r.act(pt[:], psr[:, 0:NE], AF.Exp, bias=nm[:])
        r.v("tensor_reduce", sm[:], pt[:], AX, ALU.add)
        r.v("reciprocal", sm[:], sm[:])
        r.v("tensor_scalar_mul", pt[:], pt[:], sm[:])
        r.v("reduce_max", v1[:], pt[:], AX)
        r.v("tensor_single_scalar", p2[:], pt[:], v1[:], ALU.is_ge)
        r.v("scalar_tensor_tensor", p2[:], p2[:], -1e30, pt[:], ALU.mult, ALU.add)
        r.v("reduce_max", v2[:], p2[:], AX)
        r.v("tensor_tensor", sm[:], v1[:], v2[:], ALU.add)
        r.v("reciprocal", sm[:], sm[:])
        r.v("tensor_single_scalar", twt[:], pt[:], v2[:], ALU.is_ge)
        r.v("tensor_tensor", twt[:], twt[:], pt[:], ALU.mult)
        r.v("tensor_scalar_mul", twt[:], twt[:], sm[:])
        r.dma(TW[tc * 128:(tc + 1) * 128, :], twt[:])


def emit_moe(ph):
    r = ph.r
    EIT = ph.IN["EIT"]
    TW = ph.IN["TW"]
    MOE = ph.OUT["MOE"]
    c = emit_const_tiles(ph)

    ewk_bf = ph.dram("ewk_bf", [NE * D, FE], BF16)
    ewv_bf = ph.dram("ewv_bf", [NE * FE, D], BF16)
    ewr_bf = ph.dram("ewr_bf", [NE * D, D], BF16)
    r.dma(ewk_bf, ph.IN["eWk"].rearrange("e d f -> (e d) f"), eng="gpsimd")
    r.dma(ewv_bf, ph.IN["eWv"].rearrange("e f d -> (e f) d"), eng="gpsimd")
    r.dma(ewr_bf, ph.IN["eWr"].rearrange("e d g -> (e d) g"), eng="gpsimd")

    cmk_flat = ph.IN["cm_mu_k"].rearrange("e d -> (e d)")
    cmr_flat = ph.IN["cm_mu_r"].rearrange("e d -> (e d)")
    pscr = ph.ps("pscr", [128, 512], F32)
    muks = [emit_scale_cols(ph, r, c, cmk_flat[e * D:(e + 1) * D], D, f"cmk{e}", pscr)
            for e in range(NE)]
    murs = [emit_scale_cols(ph, r, c, cmr_flat[e * D:(e + 1) * D], D, f"cmr{e}", pscr)
            for e in range(NE)]

    EXXf = ph.dram("EXXf", [D, T], BF16)
    DFMf = ph.dram("DFMf", [D, T], BF16)
    abuf = ph.sb("abuf", [128, S + 16], BF16)
    dtmp = ph.sb("dtmp", [128, S], BF16)
    r.g("memset", abuf[:, 15:16], 0.0)
    for cc in range(8):
        for s in range(NSEQ):
            r.dma(abuf[:, 16:16 + S],
                  EIT[s * S:(s + 1) * S, cc * 128:(cc + 1) * 128], transpose=True)
            r.dma(EXXf[cc * 128:(cc + 1) * 128, s * S:(s + 1) * S],
                  abuf[:, 15:15 + S])
            r.v("tensor_tensor", dtmp[:], abuf[:, 16:16 + S], abuf[:, 15:15 + S],
                ALU.subtract)
            r.dma(DFMf[cc * 128:(cc + 1) * 128, s * S:(s + 1) * S], dtmp[:])

    twsb = ph.sb("twsb", [128, (T // 128) * NE], F32)
    for tc in range(T // 128):
        r.dma(twsb[:, tc * NE:(tc + 1) * NE], TW[tc * 128:(tc + 1) * 128, :])

    xke = [ph.sb(f"xke{k}", [128, S], BF16) for k in range(8)]
    ke = [ph.sb(f"ke{k}", [128, S], BF16) for k in range(FE // 128)]
    exx_t = ph.sb("exx_t", [128, S], BF16)
    dfm_t = ph.sb("dfm_t", [128, S], BF16)
    wkm = ph.sb("wkm", [128, 8 * 128], BF16)
    wvsb = ph.sb("wvsb", [128, (FE // 128) * D], BF16)
    wrsb = ph.sb("wrsb", [128, 8 * D], BF16)
    relu_t = ph.sb("relu_t", [128, 512], F32)
    sig_t = ph.sb("sig_t", [128, 512], F32)
    kvr_t = ph.sb("kvr_t", [128, 512], F32)
    out_t = ph.sb("out_t", [128, 512], F32)
    ps_ke = ph.ps("ps_ke", [128, 512], F32)
    ps_r = ph.ps("ps_r", [128, 512], F32)
    ps_kv = ph.ps("ps_kv", [128, 512], F32)

    for e in range(NE):
        for s in range(NSEQ):
            cols = slice(s * S, (s + 1) * S)
            for k in range(8):
                r.dma(exx_t[:], EXXf[k * 128:(k + 1) * 128, cols])
                r.dma(dfm_t[:], DFMf[k * 128:(k + 1) * 128, cols])
                r.v("scalar_tensor_tensor", xke[k][:], dfm_t[:],
                    muks[e][:, k:k + 1], exx_t[:], ALU.mult, ALU.add)
            for m in range(FE // 128):
                r.dma(wkm[:].rearrange("p (k o) -> p k o", k=8),
                      ewk_bf[e * D:(e + 1) * D, m * 128:(m + 1) * 128]
                      .rearrange("(k p) o -> p k o", p=128))
                for n in range(S // 512):
                    nt = slice(n * 512, (n + 1) * 512)
                    for k in range(8):
                        r.mm(ps_ke[:], wkm[:, k * 128:(k + 1) * 128],
                             xke[k][:, nt], start=(k == 0), stop=(k == 7))
                    r.act(relu_t[:], ps_ke[:], AF.Relu)
                    r.v("tensor_tensor", ke[m][:, nt], relu_t[:], relu_t[:],
                        ALU.mult)
            for k in range(8):
                r.dma(exx_t[:], EXXf[k * 128:(k + 1) * 128, cols])
                r.dma(dfm_t[:], DFMf[k * 128:(k + 1) * 128, cols])
                r.v("scalar_tensor_tensor", xke[k][:], dfm_t[:],
                    murs[e][:, k:k + 1], exx_t[:], ALU.mult, ALU.add)
            for k in range(FE // 128):
                r.dma(wvsb[:, k * D:(k + 1) * D],
                      ewv_bf[e * FE + k * 128:e * FE + (k + 1) * 128, :])
            for k in range(8):
                r.dma(wrsb[:, k * D:(k + 1) * D],
                      ewr_bf[e * D + k * 128:e * D + (k + 1) * 128, :])
            for tc in range(S // 128):
                tcg = s * (S // 128) + tc
                tsl = slice(tc * 128, (tc + 1) * 128)
                for n in range(2):
                    for k in range(8):
                        r.mm(ps_r[:], xke[k][:, tsl],
                             wrsb[:, k * D + n * 512:k * D + (n + 1) * 512],
                             start=(k == 0), stop=(k == 7))
                    r.act(sig_t[:], ps_r[:], AF.Sigmoid)
                    for k in range(FE // 128):
                        r.mm(ps_kv[:], ke[k][:, tsl],
                             wvsb[:, k * D + n * 512:k * D + (n + 1) * 512],
                             start=(k == 0), stop=(k == FE // 128 - 1))
                    r.v("tensor_tensor", kvr_t[:], sig_t[:], ps_kv[:], ALU.mult)
                    r.v("tensor_scalar_mul", out_t[:], kvr_t[:],
                        twsb[:, tcg * NE + e:tcg * NE + e + 1])
                    if e == 0:
                        r.dma(MOE[tcg * 128:(tcg + 1) * 128,
                                  n * 512:(n + 1) * 512], out_t[:], eng="gpsimd")
                    else:
                        r.dma(MOE[tcg * 128:(tcg + 1) * 128,
                                  n * 512:(n + 1) * 512], out_t[:], eng="gpsimd",
                              accum_op=ALU.add)


def emit_ffn1(ph):
    r = ph.r
    XN1 = ph.IN["XN1"]
    ACC = ph.IN["ACC"]
    MOE = ph.IN["MOE"]
    OUT = ph.OUT["OUT"]
    c = emit_const_tiles(ph)

    wsb = ph.sb("wsb", [128, 8 * D], F32)
    for k in range(8):
        r.dma(wsb[:, k * D:(k + 1) * D],
              ph.IN["weight"][k * 128:(k + 1) * 128, :])
    asum = ph.sb("asum", [128, 1], F32)
    at = ph.sb("at", [128, 1], F32)
    r.v("tensor_reduce", asum[:], wsb[:, 0:D], AX, ALU.add, True)
    for k in range(1, 8):
        r.v("tensor_reduce", at[:], wsb[:, k * D:(k + 1) * D], AX, ALU.add, True)
        r.v("tensor_tensor", asum[:], asum[:], at[:], ALU.add)
    ps_s = ph.ps("ps_s", [128, 512], F32)
    r.mm(ps_s[0:1, 0:1], c["ones_c_f"][:], asum[:])
    beta11 = ph.sb("beta11", [1, 1], F32)
    r.v("tensor_scalar_mul", beta11[:], ps_s[0:1, 0:1], 1.0 / (D * D))
    beta128 = ph.sb("beta128", [128, 1], F32)
    r.mm(ps_s[0:128, 0:1], c["ones_r_f"][:], beta11[:])
    r.v("tensor_copy", beta128[:], ps_s[0:128, 0:1])
    brec = ph.sb("brec", [128, 1], F32)
    r.v("tensor_single_scalar", brec[:], beta128[:], 1e-5, ALU.add)
    r.v("reciprocal", brec[:], brec[:])
    wqsb = ph.sb("wqsb", [128, 8 * D], BF16)
    zt = ph.sb("zt", [128, D], F32)
    m1t = ph.sb("m1t", [128, D], F32)
    for k in range(8):
        ksl = slice(k * D, (k + 1) * D)
        r.v("tensor_scalar_mul", zt[:], wsb[:, ksl], brec[:])
        r.v("tensor_single_scalar", m1t[:], zt[:], 0.5, ALU.is_gt)
        r.v("tensor_single_scalar", zt[:], zt[:], -0.5, ALU.is_lt)
        r.v("tensor_tensor", wqsb[:, ksl], m1t[:], zt[:], ALU.subtract)

    gl_row = ph.sb("gl_row", [1, D], F32)
    bl_row = ph.sb("bl_row", [1, D], F32)
    r.dma(gl_row[:], ph.IN["ln_gamma"].rearrange("(o d) -> o d", o=1))
    r.dma(bl_row[:], ph.IN["ln_beta"].rearrange("(o d) -> o d", o=1))
    GLt = ph.sb("GLt", [128, D], F32)
    BLt = ph.sb("BLt", [128, D], F32)
    emit_row_bcast(ph, r, c, gl_row[:], GLt, D, ps_s)
    emit_row_bcast(ph, r, c, bl_row[:], BLt, D, ps_s)
    XNI = ph.dram("XNI", [T, D], BF16)
    xt = ph.sb("xt", [128, D], F32)
    xnit = ph.sb("xnit", [128, D], F32)
    gacc = ph.sb("gacc", [128, 1], F32)
    gtmp = ph.sb("gtmp", [128, 1], F32)
    r.g("memset", gacc[:], 0.0)
    scr = {"sq": ph.sb("sq", [128, D], F32),
           "s1": ph.sb("s1", [128, 1], F32),
           "s2": ph.sb("s2", [128, 1], F32),
           "s3": ph.sb("s3", [128, 1], F32),
           "xn": ph.sb("xnscr", [128, D], F32)}
    xnib = ph.sb("xnib", [128, D], BF16)
    for tc in range(T // 128):
        rows = slice(tc * 128, (tc + 1) * 128)
        r.dma(xt[:], XN1[rows, :])
        emit_ln_tile(ph, r, xt[:], xnit[:], scr, 1e-3, gb_tiles=(GLt, BLt))
        r.v("tensor_copy", xnib[:], xnit[:])
        r.dma(XNI[rows, :], xnib[:])
        r.v("reduce_max", gtmp[:], xnit[:], AX, True)
        r.v("tensor_tensor", gacc[:], gacc[:], gtmp[:], ALU.max)

    g1x1 = ph.sb("g1x1", [1, 1], F32)
    g1x128 = ph.sb("g1x128", [1, 128], F32)
    r.tr(ps_s[0:1, 0:128], gacc[:], c["id_f"][:])
    r.v("reduce_max", g1x1[:], ps_s[0:1, 0:128], AX)
    r.v("tensor_copy", g1x128[:], g1x1[0:1, 0:1].broadcast_to([1, 128]))
    CCIN = ph.dram("CCIN", [1, 128], F32)
    CCOUT = ph.dram("CCOUT", [1, 128], F32)
    r.dma(CCIN, g1x128[:], eng="gpsimd")
    r.comp("gpsimd", "collective_compute", "AllReduce", ALU.max,
           [[0, 1, 2, 3, 4, 5, 6, 7]], [CCIN], [CCOUT])
    gam11 = ph.sb("gam11", [1, 1], F32)
    r.dma(gam11[:], CCOUT[0:1, 0:1])
    gam128 = ph.sb("gam128", [128, 1], F32)
    r.mm(ps_s[0:128, 0:1], c["ones_r_f"][:], gam11[:])
    r.v("tensor_copy", gam128[:], ps_s[0:128, 0:1])
    sxq = ph.sb("sxq", [128, 1], F32)
    r.v("reciprocal", sxq[:], gam128[:])
    r.v("tensor_scalar_mul", sxq[:], sxq[:], 128.0)
    sy = ph.sb("sy", [128, 1], F32)
    r.v("tensor_tensor", sy[:], gam128[:], beta128[:], ALU.mult)
    r.v("tensor_scalar_mul", sy[:], sy[:], 1.0 / 128.0)

    CLIP = 128.0 - 1e-5
    xqf = [ph.sb(f"xqf{k}", [128, T], BF16) for k in range(8)]
    for k in range(8):
        r.dma(xqf[k][:], XNI[0:T, k * 128:(k + 1) * 128], transpose=True)
        r.v("tensor_scalar_mul", xqf[k][:], xqf[k][:], sxq[:])
        r.v("tensor_single_scalar", xqf[k][:], xqf[k][:], CLIP, ALU.min)
        r.v("tensor_single_scalar", xqf[k][:], xqf[k][:], -CLIP, ALU.max)
    psy = ph.ps("psy", [128, 512], F32)
    yt = ph.sb("yt", [128, D], F32)
    acct = ph.sb("acct", [128, D], F32)
    moet = ph.sb("moet", [128, D], F32)
    for tc in range(T // 128):
        rows = slice(tc * 128, (tc + 1) * 128)
        for n in range(2):
            nt = slice(n * 512, (n + 1) * 512)
            for k in range(8):
                r.mm(psy[:], xqf[k][:, tc * 128:(tc + 1) * 128],
                     wqsb[:, k * D + n * 512:k * D + (n + 1) * 512],
                     start=(k == 0), stop=(k == 7))
            r.v("tensor_scalar_mul", yt[:, nt], psy[:], sy[:])
        r.dma(acct[:], ACC[rows, :])
        r.dma(moet[:], MOE[rows, :])
        r.v("tensor_tensor", yt[:], yt[:], acct[:], ALU.add)
        r.v("tensor_tensor", yt[:], yt[:], moet[:], ALU.add)
        r.dma(OUT[rows, :], yt[:])


def kernel_unfused(**inputs):
    x = np.ascontiguousarray(inputs["L0_attn_x"], dtype=np.float32)
    xs = [np.ascontiguousarray(x[2 * i:2 * i + 2].reshape(T, D)) for i in range(NCORES)]
    inp = inputs

    ph = Ph("ln0")
    ph.inp("x", xs)
    ph.out("XN0T", [T, D], BF16)
    emit_ln0(ph)
    XN0T = [o["XN0T"] for o in ph.run()]

    ph = Ph("attn0")
    ph.inp("XN0T", XN0T)
    ph.inp("w_qkv", inp["L0_attn_w_qkv"])
    ph.inp("fc_code", inp["L0_attn_fc_code"])
    ph.inp("w_out", inp["L0_attn_w_out"])
    ph.out("AO", [T, D], F32)
    emit_attn0(ph)
    AO = [o["AO"] for o in ph.run()]

    ph = Ph("ffn0a")
    ph.inp("XN0T", XN0T)
    ph.inp("w1_w", inp["L0_ffn_w1_w"])
    ph.inp("w1_s", inp["L0_ffn_w1_s"])
    ph.inp("w2_w", inp["L0_ffn_w2_w"])
    ph.inp("w2_s", inp["L0_ffn_w2_s"])
    ph.out("H", [DH, T], BF16)
    emit_ffn0a(ph)
    H = [o["H"] for o in ph.run()]

    ph = Ph("ffn0b")
    ph.inp("H", H)
    ph.inp("h1_w", inp["L0_ffn_h1_w"])
    ph.inp("h1_s", inp["L0_ffn_h1_s"])
    ph.out("G1", [DH, T], BF16)
    emit_ffn0bc(ph, "h1_w", "h1_s", "H", "G1")
    G1 = [o["G1"] for o in ph.run()]
    del H

    ph = Ph("ffn0c")
    ph.inp("G1", G1)
    ph.inp("h2_w", inp["L0_ffn_h2_w"])
    ph.inp("h2_s", inp["L0_ffn_h2_s"])
    ph.out("G2", [DH, T], BF16)
    emit_ffn0bc(ph, "h2_w", "h2_s", "G1", "G2")
    G2 = [o["G2"] for o in ph.run()]
    del G1

    ph = Ph("ffn0d")
    ph.inp("G2", G2)
    ph.inp("w3_w", inp["L0_ffn_w3_w"])
    ph.inp("w3_s", inp["L0_ffn_w3_s"])
    ph.inp("x", xs)
    ph.inp("AO", AO)
    ph.out("X1", [T, D], F32)
    emit_ffn0d(ph)
    X1 = [o["X1"] for o in ph.run()]
    del G2, AO

    ph = Ph("l1pre")
    ph.inp("X1", X1)
    for nm in ("ln1_g", "ln1_b", "tm_mu_k", "tm_mu_v", "tm_mu_r",
               "Wk", "Wv", "Wr"):
        ph.inp(nm, inp["L1_attn_" + nm])
    ph.out("XN1", [T, D], F32)
    ph.out("E", [D, T], F32)
    ph.out("V1", [D, T], F32)
    ph.out("R", [D, T], F32)
    emit_l1pre(ph)
    o = ph.run()
    XN1 = [q["XN1"] for q in o]
    E = [q["E"] for q in o]
    V1 = [q["V1"] for q in o]
    R = [q["R"] for q in o]

    ph = Ph("scan")
    ph.inp("E", E)
    ph.inp("V1", V1)
    ph.inp("R", R)
    ph.inp("XN1", XN1)
    ph.inp("X1", X1)
    for nm in ("time_decay", "time_first", "Wo", "ln2_g", "ln2_b"):
        ph.inp(nm, inp["L1_attn_" + nm])
    ph.out("ACC", [T, D], F32)
    ph.out("EIT", [T, D], BF16)
    emit_scan(ph)
    o = ph.run()
    ACC = [q["ACC"] for q in o]
    EIT = [q["EIT"] for q in o]
    del E, V1, R, X1

    ph = Ph("router")
    ph.inp("EIT", EIT)
    ph.inp("router_w", inp["L1_attn_router_w"])
    ph.out("TW", [T, NE], F32)
    emit_router(ph)
    TW = [q["TW"] for q in ph.run()]

    ph = Ph("moe")
    ph.inp("EIT", EIT)
    ph.inp("TW", TW)
    for nm in ("cm_mu_k", "cm_mu_r", "eWk", "eWv", "eWr"):
        ph.inp(nm, inp["L1_attn_" + nm])
    ph.out("MOE", [T, D], F32)
    emit_moe(ph)
    MOE = [q["MOE"] for q in ph.run()]

    ph = Ph("ffn1")
    ph.inp("XN1", XN1)
    ph.inp("ACC", ACC)
    ph.inp("MOE", MOE)
    ph.inp("ln_gamma", inp["L1_ffn_ln_gamma"])
    ph.inp("ln_beta", inp["L1_ffn_ln_beta"])
    ph.inp("weight", inp["L1_ffn_weight"])
    ph.out("OUT", [T, D], F32)
    emit_ffn1(ph)
    OUT = [q["OUT"] for q in ph.run()]

    return np.concatenate([np.asarray(a, np.float32).reshape(2, S, D)
                           for a in OUT], axis=0)



_WMAP = {
    "w_qkv": "L0_attn_w_qkv", "fc_code": "L0_attn_fc_code",
    "w_out": "L0_attn_w_out",
    "w1_w": "L0_ffn_w1_w", "w1_s": "L0_ffn_w1_s",
    "w2_w": "L0_ffn_w2_w", "w2_s": "L0_ffn_w2_s",
    "h1_w": "L0_ffn_h1_w", "h1_s": "L0_ffn_h1_s",
    "h2_w": "L0_ffn_h2_w", "h2_s": "L0_ffn_h2_s",
    "w3_w": "L0_ffn_w3_w", "w3_s": "L0_ffn_w3_s",
    "ln1_g": "L1_attn_ln1_g", "ln1_b": "L1_attn_ln1_b",
    "ln2_g": "L1_attn_ln2_g", "ln2_b": "L1_attn_ln2_b",
    "tm_mu_k": "L1_attn_tm_mu_k", "tm_mu_v": "L1_attn_tm_mu_v",
    "tm_mu_r": "L1_attn_tm_mu_r",
    "time_decay": "L1_attn_time_decay", "time_first": "L1_attn_time_first",
    "Wk": "L1_attn_Wk", "Wv": "L1_attn_Wv", "Wr": "L1_attn_Wr",
    "Wo": "L1_attn_Wo",
    "router_w": "L1_attn_router_w",
    "cm_mu_k": "L1_attn_cm_mu_k", "cm_mu_r": "L1_attn_cm_mu_r",
    "eWk": "L1_attn_eWk", "eWv": "L1_attn_eWv", "eWr": "L1_attn_eWr",
    "weight": "L1_ffn_weight",
    "ln_gamma": "L1_ffn_ln_gamma", "ln_beta": "L1_ffn_ln_beta",
}


def kernel(**inputs):
    x = np.ascontiguousarray(inputs["L0_attn_x"], dtype=np.float32)
    xs = [np.ascontiguousarray(x[2 * i:2 * i + 2].reshape(T, D))
          for i in range(NCORES)]

    ph = Ph("mega")
    W = {"x": ph.inp("x", xs)}
    for short, long in _WMAP.items():
        W[short] = ph.inp(short, np.ascontiguousarray(inputs[long], dtype=np.float32))
    OUTap = ph.out("OUT", [T, D], F32)

    XN0T = ph.dram("XN0Td", [T, D], BF16)
    AOd = ph.dram("AOd", [T, D], F32)
    Hd = ph.dram("Hd", [DH, T], BF16)
    G1d = ph.dram("G1d", [DH, T], BF16)
    G2d = ph.dram("G2d", [DH, T], BF16)
    X1d = ph.dram("X1d", [T, D], F32)
    XN1d = ph.dram("XN1d", [T, D], F32)
    Ed = ph.dram("Ed", [D, T], F32)
    V1d = ph.dram("V1d", [D, T], F32)
    Rd = ph.dram("Rd", [D, T], F32)
    ACCd = ph.dram("ACCd", [T, D], F32)
    EITd = ph.dram("EITd", [T, D], BF16)
    TWd = ph.dram("TWd", [T, NE], F32)
    MOEd = ph.dram("MOEd", [T, D], F32)

    def sec(prefix, emit_fn, IN, OUTm, *args):
        ph.begin(prefix)
        ph.IN = IN
        ph.OUT = OUTm
        emit_fn(ph, *args)
        ph.end()

    sec("p0", emit_ln0, {"x": W["x"]}, {"XN0T": XN0T})
    sec("p1", emit_attn0,
        {"XN0T": XN0T, "w_qkv": W["w_qkv"], "fc_code": W["fc_code"],
         "w_out": W["w_out"]}, {"AO": AOd})
    sec("p2", emit_ffn0a,
        {"XN0T": XN0T, "w1_w": W["w1_w"], "w1_s": W["w1_s"],
         "w2_w": W["w2_w"], "w2_s": W["w2_s"]}, {"H": Hd})
    sec("p3", emit_ffn0bc,
        {"H": Hd, "h1_w": W["h1_w"], "h1_s": W["h1_s"]}, {"G1": G1d},
        "h1_w", "h1_s", "H", "G1")
    sec("p4", emit_ffn0bc,
        {"G1": G1d, "h2_w": W["h2_w"], "h2_s": W["h2_s"]}, {"G2": G2d},
        "h2_w", "h2_s", "G1", "G2")
    sec("p5", emit_ffn0d,
        {"G2": G2d, "w3_w": W["w3_w"], "w3_s": W["w3_s"], "x": W["x"],
         "AO": AOd}, {"X1": X1d})
    sec("p6", emit_l1pre,
        {"X1": X1d, "ln1_g": W["ln1_g"], "ln1_b": W["ln1_b"],
         "tm_mu_k": W["tm_mu_k"], "tm_mu_v": W["tm_mu_v"],
         "tm_mu_r": W["tm_mu_r"], "Wk": W["Wk"], "Wv": W["Wv"],
         "Wr": W["Wr"]},
        {"XN1": XN1d, "E": Ed, "V1": V1d, "R": Rd})
    sec("p7", emit_scan,
        {"E": Ed, "V1": V1d, "R": Rd, "XN1": XN1d, "X1": X1d,
         "time_decay": W["time_decay"], "time_first": W["time_first"],
         "Wo": W["Wo"], "ln2_g": W["ln2_g"], "ln2_b": W["ln2_b"]},
        {"ACC": ACCd, "EIT": EITd})
    sec("p8", emit_router, {"EIT": EITd, "router_w": W["router_w"]},
        {"TW": TWd})
    sec("p9", emit_moe,
        {"EIT": EITd, "TW": TWd, "cm_mu_k": W["cm_mu_k"],
         "cm_mu_r": W["cm_mu_r"], "eWk": W["eWk"], "eWv": W["eWv"],
         "eWr": W["eWr"]}, {"MOE": MOEd})
    sec("p10", emit_ffn1,
        {"XN1": XN1d, "ACC": ACCd, "MOE": MOEd, "ln_gamma": W["ln_gamma"],
         "ln_beta": W["ln_beta"], "weight": W["weight"]}, {"OUT": OUTap})

    OUT = [q["OUT"] for q in ph.run()]
    return np.concatenate([np.asarray(a, np.float32).reshape(2, S, D)
                           for a in OUT], axis=0)


# version 08
# speedup vs baseline: 1.9630x; 1.3336x over previous
import numpy as np
import ml_dtypes
from contextlib import ExitStack

import concourse.bass as bass
import concourse.mybir as mybir
from concourse.bass_utils import run_bass_kernel_spmd

F32 = mybir.dt.float32
BF16 = mybir.dt.bfloat16
AF = mybir.ActivationFunctionType
ALU = mybir.AluOpType
AX = mybir.AxisListType.X

NCORES = 8
NSEQ = 2
S = 2048
T = NSEQ * S
D = 1024
DH = 4096
NH = 16
HD = 64
CS = 16
NE = 8
FE = 2048

BF = ml_dtypes.bfloat16


def np2dt(dt):
    if dt == np.float32:
        return F32
    if dt == BF:
        return BF16
    raise ValueError(dt)


class Rec:

    ENGINES = ("sync", "scalar", "vector", "gpsimd", "tensor")
    NDMA = 16

    RW = {
        "matmul": ((0,), (1, 2)),
        "transpose": ((0,), (1, 2)),
        "activation": ((0,), (1,)),
        "tensor_copy": ((0,), (1,)),
        "tensor_tensor": ((0,), (1, 2)),
        "tensor_add": ((0,), (1, 2)),
        "tensor_sub": ((0,), (1, 2)),
        "tensor_mul": ((0,), (1, 2)),
        "tensor_max": ((0,), (1, 2)),
        "tensor_scalar": ((0,), (1, 2, 3)),
        "tensor_single_scalar": ((0,), (1, 2)),
        "tensor_scalar_mul": ((0,), (1, 2)),
        "tensor_scalar_add": ((0,), (1, 2)),
        "tensor_scalar_max": ((0,), (1, 2)),
        "tensor_scalar_min": ((0,), (1, 2)),
        "tensor_reduce": ((0,), (1,)),
        "reduce_max": ((0,), (1,)),
        "reduce_sum": ((0,), (1,)),
        "reciprocal": ((0,), (1,)),
        "scalar_tensor_tensor": ((0,), (1, 2, 3)),
        "tensor_tensor_scan": ((0,), (1, 2, 3)),
        "memset": ((0,), ()),
        "select": ((0,), (1, 2, 3)),
    }

    def __init__(self, nc, es):
        self.esem = {e: es.enter_context(nc.semaphore("sem_" + e))
                     for e in self.ENGINES}
        self.dsems = [es.enter_context(nc.semaphore(f"dsem{i}"))
                      for i in range(self.NDMA)]
        self.streams = {e: [] for e in self.ENGINES}
        self.ec = {e: 0 for e in self.ENGINES}
        self.dc = [0] * self.NDMA
        self.dtot = 0
        self.guar = {e: {} for e in self.ENGINES}
        self.lastw = {}
        self.readers = {}
        self.pending_bar = {e: None for e in self.ENGINES}
        self.psum_names = set()
        self.conservative = False

    @staticmethod
    def _aps(args):
        out = []
        for a in args:
            t = getattr(a, "tensor", None)
            if t is not None:
                out.append(t.name)
        return out

    def _wait(self, eng, key, cnt):
        if self.guar[eng].get(key, 0) >= cnt:
            return
        if key[0] == "e":
            if key[1] == eng and eng == "tensor":
                return
            self.streams[eng].append(
                ("wait_ge", (self.esem[key[1]], cnt), {}, None))
        else:
            cnt = self.dc[key[1]]
            if self.guar[eng].get(key, 0) >= cnt:
                return
            self.streams[eng].append(
                ("wait_ge", (self.dsems[key[1]], cnt * 16), {}, None))
        self.guar[eng][key] = cnt

    def _flush_barrier(self, eng):
        bar = self.pending_bar[eng]
        if bar is None:
            return
        ecs, dcs = bar
        for e2, cnt in ecs.items():
            if cnt:
                self._wait(eng, ("e", e2), cnt)
        for i, cnt in enumerate(dcs):
            if cnt:
                self._wait(eng, ("d", i), cnt)
        self.pending_bar[eng] = None

    def _deps(self, eng, wset, rset):
        self._flush_barrier(eng)
        if self.conservative:
            for e2 in self.ENGINES:
                if self.ec[e2]:
                    self._wait(eng, ("e", e2), self.ec[e2])
            self._dma_quiesce(eng)
            return
        deps = {}
        for t in rset:
            for key, cnt in self.lastw.get(t, {}).items():
                deps[key] = max(deps.get(key, 0), cnt)
        for t in wset:
            for key, cnt in self.readers.get(t, {}).items():
                deps[key] = max(deps.get(key, 0), cnt)
        for key, cnt in deps.items():
            self._wait(eng, key, cnt)

    def _record(self, tok, wset, rset):
        key, cnt = tok
        for t in wset:
            self.lastw.setdefault(t, {})[key] = cnt
        for t in rset:
            self.readers.setdefault(t, {})[key] = cnt

    def _dma_quiesce(self, eng):
        for i in range(self.NDMA):
            if self.dc[i]:
                self._wait(eng, ("d", i), self.dc[i])

    def dma(self, dst, src, eng="sync", **kw):
        wset = self._aps([dst])
        rset = self._aps([src])
        if kw.get("accum_op") is not None:
            rset = rset + wset
        self._deps(eng, wset, rset)
        if kw.get("transpose"):
            self._dma_quiesce(eng)
        else:
            lt = getattr(self, "_last_trans", None)
            if lt is not None:
                self._wait(eng, lt, 0)
        i = self.dtot % self.NDMA
        if self.dc[i]:
            self._wait(eng, ("d", i), self.dc[i])
        self.dtot += 1
        self.dc[i] += 1
        self.streams[eng].append(
            ("dma_start", (dst, src), kw, (self.dsems[i], 16)))
        if kw.get("transpose"):
            self._last_trans = ("d", i)
        self._record((("d", i), self.dc[i]), wset, rset)

    def comp(self, eng, meth, *args, **kw):
        if meth == "affine_select":
            wset = self._aps([kw.get("out")])
            rset = self._aps([kw.get("in_")])
        elif meth == "collective_compute":
            wset = self._aps(args[4])
            rset = self._aps(args[3])
        else:
            wpos, rpos = self.RW[meth]
            wset = self._aps([args[p] for p in wpos if p < len(args)])
            rset = self._aps([args[p] for p in rpos if p < len(args)])
            for kwname in ("bias", "scale", "alpha", "mask"):
                if kwname in kw:
                    rset += self._aps([kw[kwname]])
            if kw.get("accum_out") is not None:
                wset += self._aps([kw["accum_out"]])
        for t in list(rset):
            if t in self.psum_names and t not in wset:
                wset = wset + [t]
        self._deps(eng, wset, rset)
        self.ec[eng] += 1
        self.streams[eng].append((meth, args, kw, (self.esem[eng], 1)))
        self._record((("e", eng), self.ec[eng]), wset, rset)

    def barrier(self):
        snap = (dict(self.ec), list(self.dc))
        for e in self.ENGINES:
            self.pending_bar[e] = snap

    def finish(self):
        st = self.streams["sync"]
        for e in self.ENGINES:
            if self.ec[e]:
                st.append(("wait_ge", (self.esem[e], self.ec[e]), {}, None))
        for i in range(self.NDMA):
            if self.dc[i]:
                st.append(("wait_ge", (self.dsems[i], self.dc[i] * 16), {}, None))

    def flush_into(self, block):
        for eng in self.ENGINES:
            stream = self.streams[eng]
            if not stream:
                continue

            def make(fn_stream):
                def _(engobj):
                    for meth, args, kw, inc in fn_stream:
                        ins = getattr(engobj, meth)(*args, **kw)
                        if inc is not None:
                            ins.then_inc(*inc)
                return _

            getattr(block, eng)(make(stream))
        self.streams = {e: [] for e in self.ENGINES}

    def mm(self, out, lhsT, rhs, start=True, stop=True):
        self.comp("tensor", "matmul", out, lhsT, rhs, start=start, stop=stop)

    def tr(self, out_psum, in_, ident):
        self.comp("tensor", "transpose", out_psum, in_, ident)

    def act(self, out, in_, func, **kw):
        self.comp("scalar", "activation", out, in_, func, **kw)

    def v(self, meth, *args, **kw):
        self.comp("vector", meth, *args, **kw)

    def g(self, meth, *args, **kw):
        self.comp("gpsimd", meth, *args, **kw)


class Ph:

    def __init__(self, name):
        self.nc = bass.Bass("TRN2", target_bir_lowering=False, debug=False)
        self.name = name
        self.es = ExitStack()
        self.r = Rec(self.nc, self.es)
        self.IN = {}
        self.OUT = {}
        self._in_arrays = {}
        self.prefix = ""
        self.sec_es = None

    def begin(self, prefix):
        self.prefix = prefix + "_"
        self.sec_es = ExitStack()

    def end(self):
        self.r.barrier()
        with self.nc.Block() as block:
            self.r.flush_into(block)
        self.sec_es.close()
        self.sec_es = None
        self.prefix = ""

    def inp(self, name, arrs):
        if isinstance(arrs, np.ndarray):
            arrs = [arrs] * NCORES
        a0 = arrs[0]
        ap = self.nc.dram_tensor(name, list(a0.shape), np2dt(a0.dtype),
                                 kind="ExternalInput").ap()
        self.IN[name] = ap
        self._in_arrays[name] = arrs
        return ap

    def out(self, name, shape, dtype):
        ap = self.nc.dram_tensor(name, list(shape), dtype,
                                 kind="ExternalOutput").ap()
        self.OUT[name] = ap
        return ap

    def dram(self, name, shape, dtype):
        return self.nc.dram_tensor(name, list(shape), dtype, kind="Internal").ap()

    def sb(self, name, shape, dtype):
        es = self.sec_es if self.sec_es is not None else self.es
        return es.enter_context(
            self.nc.sbuf_tensor(self.prefix + name, list(shape), dtype))

    def ps(self, name, shape, dtype=F32):
        es = self.sec_es if self.sec_es is not None else self.es
        h = es.enter_context(
            self.nc.psum_tensor(self.prefix + name, list(shape), dtype))
        self.r.psum_names.add(self.prefix + name)
        return h

    def run(self, trace=False):
        self.r.finish()
        with self.nc.Block() as block:
            self.r.flush_into(block)
        self.es.close()
        in_maps = [{k: self._in_arrays[k][i] for k in self._in_arrays}
                   for i in range(NCORES)]
        res = run_bass_kernel_spmd(self.nc, in_maps,
                                   core_ids=list(range(NCORES)), trace=trace)
        return res.results



def emit_const_tiles(ph):
    r = ph.r
    c = {}
    c["id_bf"] = ph.sb("id_bf", [128, 128], BF16)
    c["id_f"] = ph.sb("id_f", [128, 128], F32)
    c["tri"] = ph.sb("tri", [128, 128], F32)
    c["ones_r_bf"] = ph.sb("ones_r_bf", [1, 128], BF16)
    c["ones_r_f"] = ph.sb("ones_r_f", [1, 128], F32)
    c["ones_c_bf"] = ph.sb("ones_c_bf", [128, 1], BF16)
    c["ones_c_f"] = ph.sb("ones_c_f", [128, 1], F32)
    for t_ in ("id_bf", "id_f"):
        r.g("memset", c[t_][:], 0.0)
        r.g("affine_select", out=c[t_][:], in_=c[t_][:],
            compare_op=ALU.not_equal, fill=1.0, base=0,
            pattern=[[-1, 128]], channel_multiplier=1)
    r.g("memset", c["tri"][:], 0.0)
    r.g("affine_select", out=c["tri"][:], in_=c["tri"][:],
        compare_op=ALU.is_gt, fill=1.0, base=0,
        pattern=[[-1, 128]], channel_multiplier=1)
    for t_ in ("ones_r_bf", "ones_r_f", "ones_c_bf", "ones_c_f"):
        r.g("memset", c[t_][:], 1.0)
    return c


def emit_ln_tile(ph, r, x_tile, out_ap, scr, eps, gb_tiles=None,
                 negmean_out=None, rstd_out=None):
    sq, s1, s2, s3 = scr["sq"], scr["s1"], scr["s2"], scr["s3"]
    r.v("tensor_reduce", s1[:], x_tile, AX, ALU.add)
    r.v("tensor_scalar_mul", s1[:], s1[:], -1.0 / D)
    r.act(sq[:], x_tile, AF.Square)
    r.v("tensor_reduce", s2[:], sq[:], AX, ALU.add)
    r.v("tensor_scalar_mul", s2[:], s2[:], 1.0 / D)
    r.v("tensor_mul", s3[:], s1[:], s1[:])
    r.v("tensor_sub", s2[:], s2[:], s3[:])
    r.v("tensor_single_scalar", s2[:], s2[:], eps, ALU.add)
    r.act(s2[:], s2[:], AF.Ln)
    r.v("tensor_scalar_mul", s2[:], s2[:], -0.5)
    r.act(s2[:], s2[:], AF.Exp)
    if negmean_out is not None:
        r.v("tensor_copy", negmean_out, s1[:])
    if rstd_out is not None:
        r.v("tensor_copy", rstd_out, s2[:])
    if gb_tiles is None:
        r.v("tensor_scalar", out_ap, x_tile, s1[:], s2[:], ALU.add, ALU.mult)
    else:
        G, B = gb_tiles
        r.v("tensor_scalar", scr["xn"][:], x_tile, s1[:], s2[:], ALU.add, ALU.mult)
        r.v("tensor_mul", scr["xn"][:], scr["xn"][:], G[:])
        r.v("tensor_tensor", out_ap, scr["xn"][:], B[:], ALU.add)


def emit_row_bcast(ph, r, c, row_f32_ap, out_tile, n, pst):
    for off in range(0, n, 512):
        w = min(512, n - off)
        r.mm(pst[:, 0:w], c["ones_r_f"][:], row_f32_ap[0:1, off:off + w])
        r.v("tensor_copy", out_tile[:, off:off + w], pst[:, 0:w])



def emit_ln0(ph):
    r = ph.r
    x = ph.IN["x"]
    xn0t = ph.OUT["XN0T"]
    xt = ph.sb("xt", [128, D], F32)
    xnb = ph.sb("xnb", [128, D], BF16)
    scr = {"sq": ph.sb("sq", [128, D], F32),
           "s1": ph.sb("s1", [128, 1], F32),
           "s2": ph.sb("s2", [128, 1], F32),
           "s3": ph.sb("s3", [128, 1], F32)}
    for tc in range(T // 128):
        r.dma(xt[:], x[tc * 128:(tc + 1) * 128, :])
        emit_ln_tile(ph, r, xt[:], xnb[:], scr, 1e-5)
        r.dma(xn0t[tc * 128:(tc + 1) * 128, :], xnb[:])



def emit_attn0(ph):
    _cons = set()
    r = ph.r
    r.conservative = True
    XN0T = ph.IN["XN0T"]
    w_qkv = ph.IN["w_qkv"]
    fc_code = ph.IN["fc_code"].rearrange("o h c d -> (o h c) d")
    w_out = ph.IN["w_out"]
    AO = ph.OUT["AO"]

    c = emit_const_tiles(ph)
    tri_bf = ph.sb("tri_bf", [128, 128], BF16)
    r.v("tensor_copy", tri_bf[:], c["tri"][:])

    wqkv_bf = ph.dram("wqkv_bf", [3 * D, D], BF16)
    WQKVT = ph.dram("WQKVT", [D, 3 * D], BF16)
    wout_bf = ph.dram("wout_bf", [D, D], BF16)
    WOUTT = ph.dram("WOUTT", [D, D], BF16)
    QKV = ph.dram("QKV", [3 * D, T], BF16)
    ph._qkv_dbg = QKV
    ph._wq_dbg = (wqkv_bf, WQKVT)
    r.dma(wqkv_bf, w_qkv, eng="gpsimd")
    r.dma(wout_bf, w_out, eng="gpsimd")
    tbig = ph.sb("tbig", [128, 3 * D], BF16)
    for k in range(8):
        r.dma(tbig[:], wqkv_bf[0:3 * D, k * 128:(k + 1) * 128], transpose=True)
        r.dma(WQKVT[k * 128:(k + 1) * 128, :], tbig[:])
    for k in range(8):
        r.dma(tbig[:, 0:D], wout_bf[0:D, k * 128:(k + 1) * 128], transpose=True)
        r.dma(WOUTT[k * 128:(k + 1) * 128, :], tbig[:, 0:D])

    ps_tr = ph.ps("ps_tr", [128, 1024], BF16)
    ps_a = ph.ps("ps_a", [128, 512], F32)
    ps_qc = ph.ps("ps_qc", [128, 512], F32)
    ps_cum = ph.ps("ps_cum", [128, 512], F32)
    ps_xo = ph.ps("ps_xo", [128, 512], F32)
    ps_M = ph.ps("ps_M", [128, 512], F32)
    ps_car = ph.ps("ps_car", [128, 512], F32)

    code_nat = ph.sb("code_nat", [16, 64], BF16)
    codeT = ph.sb("codeT", [64, 16 * NH], BF16)
    for h in range(NH):
        r.dma(code_nat[:], fc_code[h * 16:(h + 1) * 16, :], eng="gpsimd")
        r.tr(ps_tr[0:64, 0:16], code_nat[:], c["id_bf"][0:16, 0:16])
        r.v("tensor_scalar_mul", codeT[:, h * 16:(h + 1) * 16],
            ps_tr[0:64, 0:16], 0.125)

    xnf = [ph.sb(f"xnf{k}", [128, S], BF16) for k in range(8)]
    wm = ph.sb("wm", [128, 8 * 128], BF16)
    qkv_t = ph.sb("qkv_t", [128, S], BF16)
    attn_fm = [ph.sb(f"afm{k}", [128, S], BF16) for k in range(8)]
    woutt_sb = ph.sb("woutt", [128, 8 * D], BF16)
    ao_t = ph.sb("ao_t", [128, D], F32)
    for k in range(8):
        r.dma(woutt_sb[:, k * D:(k + 1) * D], WOUTT[k * 128:(k + 1) * 128, :])

    qf = ph.sb("qf", [64, S], BF16)
    kf = ph.sb("kf", [64, S], BF16)
    vf = ph.sb("vf", [64, S], BF16)
    kcm = ph.sb("kcm", [16, S], F32)
    kexp_bf = ph.sb("kexp_bf", [16, S], BF16)
    nmax = ph.sb("nmax", [16, 1], F32)
    S_tm = ph.sb("S_tm", [128, 16], BF16)
    V_tm = ph.sb("V_tm", [128, 64], BF16)
    qc = ph.sb("qc", [128, 16], F32)
    q2b = ph.sb("q2b", [128, 16], BF16)
    q2T = ph.sb("q2T", [16, 128], BF16)
    Am = ph.sb("Am", [128, 128], BF16)
    kc_sb = ph.sb("kc_sb", [128, 16], F32)
    dden = ph.sb("dden", [128, 16], F32)
    M_sb = ph.sb("M_sb", [16, 64], BF16)
    carry_bf = ph.sb("carry_bf", [1, 16], BF16)
    smax = ph.sb("smax", [128, 1], F32)
    ssum = ph.sb("ssum", [128, 1], F32)
    srec = ph.sb("srec", [128, 1], F32)

    for s in range(NSEQ):
        srow = s * S
        r.conservative = False
        for k in range(8):
            r.dma(xnf[k][:], XN0T[srow:srow + S, k * 128:(k + 1) * 128],
                  transpose=True)
        for m in range(24):
            r.dma(wm[:].rearrange("p (k o) -> p k o", k=8),
                  WQKVT[:, m * 128:(m + 1) * 128].rearrange("(k p) o -> p k o", p=128))
            for n in range(4):
                for k in range(8):
                    r.mm(ps_a[:, 0:512], wm[:, k * 128:(k + 1) * 128],
                         xnf[k][:, n * 512:(n + 1) * 512],
                         start=(k == 0), stop=(k == 7))
                r.v("tensor_copy", qkv_t[:, n * 512:(n + 1) * 512], ps_a[:, 0:512])
            r.dma(QKV[m * 128:(m + 1) * 128, srow:srow + S], qkv_t[:])

        for h in range(NH):
            r.dma(qf[:], QKV[h * HD:(h + 1) * HD, srow:srow + S])
            r.dma(kf[:], QKV[D + h * HD:D + (h + 1) * HD, srow:srow + S])
            r.dma(vf[:], QKV[2 * D + h * HD:2 * D + (h + 1) * HD, srow:srow + S])
            ct = codeT[:, h * 16:(h + 1) * 16]
            for n in range(4):
                r.mm(ps_a[0:16, 0:512], ct, kf[:, n * 512:(n + 1) * 512])
                r.v("tensor_copy", kcm[:, n * 512:(n + 1) * 512], ps_a[0:16, 0:512])
            r.v("reduce_max", nmax[:], kcm[:], AX)
            r.v("tensor_scalar_mul", nmax[:], nmax[:], -1.0)
            if "DBG_kcm" in ph.OUT and s == 0 and h == 0:
                r.dma(ph.OUT["DBG_kcm"], kcm[:])
            r.act(kcm[:], kcm[:], AF.Exp, bias=nmax[:])
            r.v("tensor_copy", kexp_bf[:], kcm[:])
            if "DBG_kexp" in ph.OUT and s == 0 and h == 0:
                r.dma(ph.OUT["DBG_kexp"], kcm[:])
            for j in range(S // 128):
                tok = slice(j * 128, (j + 1) * 128)
                last = (j == S // 128 - 1)
                if j == 0:
                    r.g("memset", M_sb[:], 0.0)
                    r.g("memset", carry_bf[:], 0.0)
                else:
                    r.v("tensor_copy", M_sb[:], ps_M[0:16, 0:64])
                    r.v("tensor_copy", carry_bf[:], ps_car[0:1, 0:16])
                r.tr(ps_tr[0:128, 0:16], kexp_bf[:, tok], c["id_bf"][0:16, 0:16])
                r.v("tensor_copy", S_tm[:], ps_tr[0:128, 0:16])
                r.tr(ps_tr[0:128, 128:192], vf[:, tok], c["id_bf"][0:64, 0:64])
                r.v("tensor_copy", V_tm[:], ps_tr[0:128, 128:192])
                r.mm(ps_cum[0:128, 0:16], tri_bf[:], S_tm[:], start=True, stop=False)
                r.mm(ps_cum[0:128, 0:16], c["ones_r_bf"][:], carry_bf[:],
                     start=False, stop=True)
                r.v("tensor_single_scalar", kc_sb[:], ps_cum[0:128, 0:16], 1e-9, ALU.add)
                r.v("reciprocal", dden[:], kc_sb[:])
                r.mm(ps_qc[0:128, 0:16], qf[:, tok], ct)
                r.v("reduce_max", smax[:], ps_qc[0:128, 0:16], AX)
                r.v("tensor_scalar_mul", smax[:], smax[:], -1.0)
                r.act(qc[:], ps_qc[0:128, 0:16], AF.Exp, bias=smax[:])
                r.v("tensor_reduce", ssum[:], qc[:], AX, ALU.add)
                r.v("reciprocal", srec[:], ssum[:])
                r.v("tensor_scalar", qc[:], qc[:], srec[:], 0.125, ALU.mult, ALU.mult)
                r.v("tensor_tensor", q2b[:], qc[:], dden[:], ALU.mult)
                r.tr(ps_tr[0:16, 256:384], q2b[:], c["id_bf"][:])
                r.v("tensor_copy", q2T[:], ps_tr[0:16, 256:384])
                r.mm(ps_a[0:128, 0:128], kexp_bf[:, tok], q2T[:])
                r.v("tensor_tensor", Am[:], ps_a[0:128, 0:128], c["tri"][:], ALU.mult)
                r.mm(ps_xo[0:64, 0:128], M_sb[:], q2T[:], start=True, stop=False)
                r.mm(ps_xo[0:64, 0:128], V_tm[:], Am[:], start=False, stop=True)
                poff = 64 * (h % 2)
                r.v("tensor_copy", attn_fm[h // 2][poff:poff + 64, tok],
                    ps_xo[0:64, 0:128])
                if "DBG_j0" in ph.OUT and s == 0 and h == 0 and j < 2:
                    dbga = ph.OUT["DBG_j0" if j == 0 else "DBG_j1"]
                    r.dma(dbga[0:128, 0:16], S_tm[:], eng="gpsimd")
                    r.dma(dbga[0:128, 16:80], V_tm[:], eng="gpsimd")
                    r.dma(dbga[0:128, 80:96], kc_sb[:])
                    r.dma(dbga[0:128, 96:112], dden[:])
                    r.dma(dbga[0:128, 112:128], q2b[:], eng="gpsimd")
                    r.dma(dbga[0:128, 128:256], Am[:], eng="gpsimd")
                    r.dma(dbga[0:16, 256:320], M_sb[:], eng="gpsimd")
                    r.dma(dbga[0:64, 320:448], attn_fm[0][0:64, tok], eng="gpsimd")
                r.mm(ps_M[0:16, 0:64], S_tm[:], V_tm[:], start=(j == 0), stop=last)
                r.mm(ps_car[0:1, 0:16], c["ones_c_bf"][:], S_tm[:], start=(j == 0), stop=last)

        if "DBG_af" in ph.OUT and s == 0:
            for kk in range(8):
                r.dma(ph.OUT["DBG_af"][0:128, kk * 2048:(kk + 1) * 2048],
                      attn_fm[kk][:], eng="gpsimd")
        for tc in range(S // 128):
            for n in range(2):
                for k in range(8):
                    r.mm(ps_a[:, 0:512], attn_fm[k][:, tc * 128:(tc + 1) * 128],
                         woutt_sb[:, k * D + n * 512:k * D + (n + 1) * 512],
                         start=(k == 0), stop=(k == 7))
                r.v("tensor_copy", ao_t[:, n * 512:(n + 1) * 512], ps_a[:, 0:512])
            r.dma(AO[srow + tc * 128:srow + (tc + 1) * 128, :], ao_t[:])



def emit_wprep_T(ph, r, src_ap, rows, cols, name):
    saved = r.conservative
    r.conservative = True
    wbf = ph.dram(name + "_bf", [rows, cols], BF16)
    wT = ph.dram(name + "_T", [cols, rows], BF16)
    r.dma(wbf, src_ap, eng="gpsimd")
    tb = ph.sb(name + "_tb", [128, rows], BF16)
    for k in range(cols // 128):
        r.dma(tb[:], wbf[0:rows, k * 128:(k + 1) * 128], transpose=True)
        r.dma(wT[k * 128:(k + 1) * 128, :], tb[:])
    r.conservative = saved
    return wT


def emit_scale_cols(ph, r, c, s_ap, n, name, pst):
    nm = n // 128
    tmp = ph.sb(name + "_ld", [nm, 128], F32)
    out = ph.sb(name + "_c", [128, nm], F32)
    r.dma(tmp[:], s_ap.rearrange("(m p) -> m p", p=128))
    r.tr(pst[0:128, 0:nm], tmp[:], c["id_f"][0:nm, 0:nm])
    r.v("tensor_copy", out[:], pst[0:128, 0:nm])
    return out


def emit_ffn0a(ph):
    r = ph.r
    XN0T = ph.IN["XN0T"]
    H = ph.OUT["H"]
    c = emit_const_tiles(ph)
    pscr = ph.ps("pscr", [128, 512], F32)
    W1T = emit_wprep_T(ph, r, ph.IN["w1_w"], DH, D, "w1")
    W2T = emit_wprep_T(ph, r, ph.IN["w2_w"], DH, D, "w2")
    s1c = emit_scale_cols(ph, r, c, ph.IN["w1_s"], DH, "s1", pscr)
    s2c = emit_scale_cols(ph, r, c, ph.IN["w2_s"], DH, "s2", pscr)

    xnf = [ph.sb(f"xnf{k}", [128, T], BF16) for k in range(8)]
    for k in range(8):
        r.dma(xnf[k][:], XN0T[0:T, k * 128:(k + 1) * 128], transpose=True)
    w1m = ph.sb("w1m", [128, 8 * 128], BF16)
    w2m = ph.sb("w2m", [128, 8 * 128], BF16)
    tsil = ph.sb("tsil", [128, 512], F32)
    tmul = ph.sb("tmul", [128, 512], F32)
    h_row = ph.sb("h_row", [128, T], BF16)
    ps1 = ph.ps("ps1", [128, 512], F32)
    ps2 = ph.ps("ps2", [128, 512], F32)
    for m in range(DH // 128):
        r.dma(w1m[:].rearrange("p (k o) -> p k o", k=8),
              W1T[:, m * 128:(m + 1) * 128].rearrange("(k p) o -> p k o", p=128))
        r.dma(w2m[:].rearrange("p (k o) -> p k o", k=8),
              W2T[:, m * 128:(m + 1) * 128].rearrange("(k p) o -> p k o", p=128))
        for n in range(T // 512):
            nt = slice(n * 512, (n + 1) * 512)
            for k in range(8):
                r.mm(ps1[:], w1m[:, k * 128:(k + 1) * 128], xnf[k][:, nt],
                     start=(k == 0), stop=(k == 7))
            for k in range(8):
                r.mm(ps2[:], w2m[:, k * 128:(k + 1) * 128], xnf[k][:, nt],
                     start=(k == 0), stop=(k == 7))
            r.act(tsil[:], ps1[:], AF.Silu, scale=s1c[:, m:m + 1])
            r.v("tensor_scalar_mul", tmul[:], ps2[:], s2c[:, m:m + 1])
            r.v("tensor_tensor", h_row[:, nt], tsil[:], tmul[:], ALU.mult)
        r.dma(H[m * 128:(m + 1) * 128, :], h_row[:])


def emit_ffn0bc(ph, win, wsn, inname, outname):
    r = ph.r
    HI = ph.IN[inname]
    GO = ph.OUT[outname]
    c = emit_const_tiles(ph)
    pscr = ph.ps("pscr", [128, 512], F32)
    WT = emit_wprep_T(ph, r, ph.IN[win], DH, DH, win)
    sc = emit_scale_cols(ph, r, c, ph.IN[wsn], DH, wsn, pscr)

    PAN = 2048
    hp = [ph.sb(f"hp{k}", [128, PAN], BF16) for k in range(DH // 128)]
    wtm = ph.sb("wtm", [128, (DH // 128) * 128], BF16)
    g_row = ph.sb("g_row", [128, PAN], BF16)
    psg = ph.ps("psg", [128, 512], F32)
    nk = DH // 128
    for p in range(T // PAN):
        pc = slice(p * PAN, (p + 1) * PAN)
        for k in range(nk):
            r.dma(hp[k][:], HI[k * 128:(k + 1) * 128, pc])
        for m in range(nk):
            r.dma(wtm[:].rearrange("p (k o) -> p k o", k=nk),
                  WT[:, m * 128:(m + 1) * 128].rearrange("(k p) o -> p k o", p=128))
            for n in range(PAN // 512):
                nt = slice(n * 512, (n + 1) * 512)
                for k in range(nk):
                    r.mm(psg[:], wtm[:, k * 128:(k + 1) * 128], hp[k][:, nt],
                         start=(k == 0), stop=(k == nk - 1))
                r.act(g_row[:, nt], psg[:], AF.Gelu, scale=sc[:, m:m + 1])
            r.dma(GO[m * 128:(m + 1) * 128, pc], g_row[:])


def emit_ffn0d(ph):
    r = ph.r
    G2 = ph.IN["G2"]
    x = ph.IN["x"]
    AO = ph.IN["AO"]
    X1 = ph.OUT["X1"]
    c = emit_const_tiles(ph)
    pscr = ph.ps("pscr", [128, 512], F32)
    W3T = emit_wprep_T(ph, r, ph.IN["w3_w"], D, DH, "w3")

    w3sb = ph.sb("w3sb", [128, (DH // 128) * D], BF16)
    for k in range(DH // 128):
        r.dma(w3sb[:, k * D:(k + 1) * D], W3T[k * 128:(k + 1) * 128, :])
    s3row = ph.sb("s3row", [1, D], F32)
    r.dma(s3row[:], ph.IN["w3_s"].rearrange("(o d) -> o d", o=1))
    W3S = ph.sb("W3S", [128, D], F32)
    emit_row_bcast(ph, r, c, s3row[:], W3S, D, pscr)

    g2m = ph.sb("g2m", [128, (DH // 128) * 128], BF16)
    yt = ph.sb("yt", [128, D], F32)
    xt = ph.sb("xt", [128, D], F32)
    aot = ph.sb("aot", [128, D], F32)
    psd = ph.ps("psd", [128, 512], F32)
    nk = DH // 128
    for tc in range(T // 128):
        r.dma(g2m[:].rearrange("p (k t) -> p k t", k=nk),
              G2[:, tc * 128:(tc + 1) * 128].rearrange("(k p) t -> p k t", p=128))
        r.dma(xt[:], x[tc * 128:(tc + 1) * 128, :])
        r.dma(aot[:], AO[tc * 128:(tc + 1) * 128, :])
        for n in range(2):
            nt = slice(n * 512, (n + 1) * 512)
            for k in range(nk):
                r.mm(psd[:], g2m[:, k * 128:(k + 1) * 128],
                     w3sb[:, k * D + n * 512:k * D + (n + 1) * 512],
                     start=(k == 0), stop=(k == nk - 1))
            r.v("tensor_tensor", yt[:, nt], psd[:], W3S[:, nt], ALU.mult)
        r.v("tensor_tensor", yt[:], yt[:], xt[:], ALU.add)
        r.v("tensor_tensor", yt[:], yt[:], aot[:], ALU.add)
        r.dma(X1[tc * 128:(tc + 1) * 128, :], yt[:])



def emit_l1pre(ph):
    r = ph.r
    X1 = ph.IN["X1"]
    XN1 = ph.OUT["XN1"]
    Ef = ph.OUT["E"]
    V1f = ph.OUT["V1"]
    Rf = ph.OUT["R"]
    c = emit_const_tiles(ph)

    g_row = ph.sb("g_row", [1, D], F32)
    b_row = ph.sb("b_row", [1, D], F32)
    r.dma(g_row[:], ph.IN["ln1_g"].rearrange("(o d) -> o d", o=1))
    r.dma(b_row[:], ph.IN["ln1_b"].rearrange("(o d) -> o d", o=1))
    pscr = ph.ps("pscr", [128, 512], F32)
    G1t = ph.sb("G1t", [128, D], F32)
    B1t = ph.sb("B1t", [128, D], F32)
    emit_row_bcast(ph, r, c, g_row[:], G1t, D, pscr)
    emit_row_bcast(ph, r, c, b_row[:], B1t, D, pscr)

    A_lnT = ph.dram("A_lnT", [T, D], BF16)
    xt = ph.sb("xt", [128, D], F32)
    xn1t = ph.sb("xn1t", [128, D], F32)
    abf = ph.sb("abf", [128, D], BF16)
    scr = {"sq": ph.sb("sq", [128, D], F32),
           "s1": ph.sb("s1", [128, 1], F32),
           "s2": ph.sb("s2", [128, 1], F32),
           "s3": ph.sb("s3", [128, 1], F32),
           "xn": ph.sb("xnscr", [128, D], F32)}
    for tc in range(T // 128):
        rows = slice(tc * 128, (tc + 1) * 128)
        r.dma(xt[:], X1[rows, :])
        emit_ln_tile(ph, r, xt[:], xn1t[:], scr, 1e-5)
        r.dma(XN1[rows, :], xn1t[:])
        emit_ln_tile(ph, r, xn1t[:], abf[:], scr, 1e-5, gb_tiles=(G1t, B1t))
        r.dma(A_lnT[rows, :], abf[:])

    muk = emit_scale_cols(ph, r, c, ph.IN["tm_mu_k"], D, "muk", pscr)
    muv = emit_scale_cols(ph, r, c, ph.IN["tm_mu_v"], D, "muv", pscr)
    mur = emit_scale_cols(ph, r, c, ph.IN["tm_mu_r"], D, "mur", pscr)

    wk_bf = ph.dram("wk_bf", [D, D], BF16)
    wv_bf = ph.dram("wv_bf", [D, D], BF16)
    wr_bf = ph.dram("wr_bf", [D, D], BF16)
    r.dma(wk_bf, ph.IN["Wk"], eng="gpsimd")
    r.dma(wv_bf, ph.IN["Wv"], eng="gpsimd")
    r.dma(wr_bf, ph.IN["Wr"], eng="gpsimd")

    xsf = [ph.sb(f"xsf{k}", [128, T], BF16) for k in range(8)]
    abuf = ph.sb("abuf", [128, S + 16], BF16)
    dbuf = ph.sb("dbuf", [128, S], BF16)
    wsb = ph.sb("wsb", [128, 8 * D], BF16)
    erow = ph.sb("erow", [128, T], F32)
    psp = ph.ps("psp", [128, 512], F32)
    r.g("memset", abuf[:, 15:16], 0.0)

    for st, (mu, wbf, outdram) in enumerate(
            [(muk, wk_bf, Ef), (muv, wv_bf, V1f), (mur, wr_bf, Rf)]):
        for cc in range(8):
            for s in range(NSEQ):
                r.dma(abuf[:, 16:16 + S],
                      A_lnT[s * S:(s + 1) * S, cc * 128:(cc + 1) * 128],
                      transpose=True)
                a_sl = abuf[:, 16:16 + S]
                xx_sl = abuf[:, 15:15 + S]
                r.v("tensor_tensor", dbuf[:], a_sl, xx_sl, ALU.subtract)
                r.v("scalar_tensor_tensor", xsf[cc][:, s * S:(s + 1) * S],
                    dbuf[:], mu[:, cc:cc + 1], xx_sl, ALU.mult, ALU.add)
        for k in range(8):
            r.dma(wsb[:, k * D:(k + 1) * D], wbf[k * 128:(k + 1) * 128, :])
        for m in range(8):
            for n in range(T // 512):
                nt = slice(n * 512, (n + 1) * 512)
                for k in range(8):
                    r.mm(psp[:], wsb[:, k * D + m * 128:k * D + (m + 1) * 128],
                         xsf[k][:, nt], start=(k == 0), stop=(k == 7))
                if st == 0:
                    r.act(erow[:, nt], psp[:], AF.Exp)
                elif st == 1:
                    r.v("tensor_copy", erow[:, nt], psp[:])
                else:
                    r.act(erow[:, nt], psp[:], AF.Sigmoid)
            r.dma(outdram[m * 128:(m + 1) * 128, :], erow[:])


def emit_scan(ph):
    r = ph.r
    Ef = ph.IN["E"]
    V1f = ph.IN["V1"]
    Rf = ph.IN["R"]
    XN1 = ph.IN["XN1"]
    X1 = ph.IN["X1"]
    ACC = ph.OUT["ACC"]
    EIT = ph.OUT["EIT"]
    c = emit_const_tiles(ph)

    pscr = ph.ps("pscr", [128, 512], F32)
    tdc = emit_scale_cols(ph, r, c, ph.IN["time_decay"], D, "tdc", pscr)
    tfc = emit_scale_cols(ph, r, c, ph.IN["time_first"], D, "tfc", pscr)
    lam = ph.sb("lam", [128, 8], F32)
    eu = ph.sb("eu", [128, 8], F32)
    r.act(lam[:], tdc[:], AF.Exp)
    r.v("tensor_scalar_mul", lam[:], lam[:], -1.0)
    r.act(lam[:], lam[:], AF.Exp)
    r.act(eu[:], tfc[:], AF.Exp)

    wo_bf = ph.dram("wo_bf", [D, D], BF16)
    r.dma(wo_bf, ph.IN["Wo"], eng="gpsimd")

    ec = ph.sb("ec", [128, S], F32)
    vc = ph.sb("vc", [128, S], F32)
    rc = ph.sb("rc", [128, S], F32)
    za = ph.sb("za", [128, S], F32)
    Ab = ph.sb("Ab", [128, S + 16], F32)
    Bb = ph.sb("Bb", [128, S + 16], F32)
    ek = ph.sb("ek", [128, S], F32)
    nb = ph.sb("nb", [128, S], F32)
    db = ph.sb("db", [128, S], F32)
    RW = [ph.sb(f"RW{k}", [128, T], BF16) for k in range(8)]
    r.g("memset", Ab[:, 15:16], 0.0)
    r.g("memset", Bb[:, 15:16], 0.0)

    for cc in range(8):
        rows = slice(cc * 128, (cc + 1) * 128)
        for s in range(NSEQ):
            cols = slice(s * S, (s + 1) * S)
            r.dma(ec[:], Ef[rows, cols])
            r.dma(vc[:], V1f[rows, cols])
            r.dma(rc[:], Rf[rows, cols])
            lamb = lam[:, cc:cc + 1].broadcast_to([128, S])
            r.v("tensor_tensor", za[:], ec[:], vc[:], ALU.mult)
            r.v("tensor_tensor_scan", Ab[:, 16:16 + S], lamb, za[:], 0.0,
                ALU.mult, ALU.add)
            r.v("tensor_tensor_scan", Bb[:, 16:16 + S], lamb, ec[:], 0.0,
                ALU.mult, ALU.add)
            r.v("tensor_scalar_mul", ek[:], ec[:], eu[:, cc:cc + 1])
            r.v("tensor_tensor", nb[:], ek[:], vc[:], ALU.mult)
            r.v("tensor_tensor", nb[:], nb[:], Ab[:, 15:15 + S], ALU.add)
            r.v("tensor_tensor", db[:], ek[:], Bb[:, 15:15 + S], ALU.add)
            r.v("reciprocal", db[:], db[:])
            r.v("tensor_tensor", nb[:], nb[:], db[:], ALU.mult)
            r.v("tensor_tensor", RW[cc][:, cols], nb[:], rc[:], ALU.mult)

    wosb = ph.sb("wosb", [128, 8 * D], BF16)
    for k in range(8):
        r.dma(wosb[:, k * D:(k + 1) * D], wo_bf[k * 128:(k + 1) * 128, :])
    g2_row = ph.sb("g2_row", [1, D], F32)
    b2_row = ph.sb("b2_row", [1, D], F32)
    r.dma(g2_row[:], ph.IN["ln2_g"].rearrange("(o d) -> o d", o=1))
    r.dma(b2_row[:], ph.IN["ln2_b"].rearrange("(o d) -> o d", o=1))
    G2t = ph.sb("G2t", [128, D], F32)
    B2t = ph.sb("B2t", [128, D], F32)
    emit_row_bcast(ph, r, c, g2_row[:], G2t, D, pscr)
    emit_row_bcast(ph, r, c, b2_row[:], B2t, D, pscr)

    psw = ph.ps("psw", [128, 512], F32)
    yt = ph.sb("yt", [128, D], F32)
    xnt = ph.sb("xnt", [128, D], F32)
    x1t = ph.sb("x1t", [128, D], F32)
    eib = ph.sb("eib", [128, D], BF16)
    scr = {"sq": ph.sb("sq", [128, D], F32),
           "s1": ph.sb("s1", [128, 1], F32),
           "s2": ph.sb("s2", [128, 1], F32),
           "s3": ph.sb("s3", [128, 1], F32),
           "xn": ph.sb("xnscr", [128, D], F32)}
    for tcg in range(T // 128):
        rows = slice(tcg * 128, (tcg + 1) * 128)
        s = tcg // (S // 128)
        tc = tcg % (S // 128)
        for n in range(2):
            nt = slice(n * 512, (n + 1) * 512)
            for k in range(8):
                r.mm(psw[:], RW[k][:, tcg * 128:(tcg + 1) * 128],
                     wosb[:, k * D + n * 512:k * D + (n + 1) * 512],
                     start=(k == 0), stop=(k == 7))
            r.v("tensor_copy", yt[:, nt], psw[:])
        r.dma(xnt[:], XN1[rows, :])
        r.dma(x1t[:], X1[rows, :])
        r.v("tensor_tensor", yt[:], yt[:], xnt[:], ALU.add)
        emit_ln_tile(ph, r, yt[:], eib[:], scr, 1e-5, gb_tiles=(G2t, B2t))
        r.dma(EIT[rows, :], eib[:])
        r.v("tensor_tensor", yt[:], yt[:], x1t[:], ALU.add)
        r.dma(ACC[rows, :], yt[:])


def emit_router(ph):
    r = ph.r
    EIT = ph.IN["EIT"]
    TW = ph.OUT["TW"]
    rw_bf = ph.dram("rw_bf", [D, NE], BF16)
    r.dma(rw_bf, ph.IN["router_w"], eng="gpsimd")
    rwsb = ph.sb("rwsb", [128, 8 * NE], BF16)
    for k in range(8):
        r.dma(rwsb[:, k * NE:(k + 1) * NE], rw_bf[k * 128:(k + 1) * 128, :])
    eif = [ph.sb(f"eif{k}", [128, T], BF16) for k in range(8)]
    for k in range(8):
        r.dma(eif[k][:], EIT[0:T, k * 128:(k + 1) * 128], transpose=True)
    psr = ph.ps("psr", [128, 512], F32)
    pt = ph.sb("pt", [128, NE], F32)
    p2 = ph.sb("p2", [128, NE], F32)
    twt = ph.sb("twt", [128, NE], F32)
    nm = ph.sb("nm", [128, 1], F32)
    sm = ph.sb("sm", [128, 1], F32)
    v1 = ph.sb("v1", [128, 1], F32)
    v2 = ph.sb("v2", [128, 1], F32)
    for tc in range(T // 128):
        for k in range(8):
            r.mm(psr[:, 0:NE], eif[k][:, tc * 128:(tc + 1) * 128],
                 rwsb[:, k * NE:(k + 1) * NE], start=(k == 0), stop=(k == 7))
        r.v("reduce_max", nm[:], psr[:, 0:NE], AX)
        r.v("tensor_scalar_mul", nm[:], nm[:], -1.0)
        r.act(pt[:], psr[:, 0:NE], AF.Exp, bias=nm[:])
        r.v("tensor_reduce", sm[:], pt[:], AX, ALU.add)
        r.v("reciprocal", sm[:], sm[:])
        r.v("tensor_scalar_mul", pt[:], pt[:], sm[:])
        r.v("reduce_max", v1[:], pt[:], AX)
        r.v("tensor_single_scalar", p2[:], pt[:], v1[:], ALU.is_ge)
        r.v("scalar_tensor_tensor", p2[:], p2[:], -1e30, pt[:], ALU.mult, ALU.add)
        r.v("reduce_max", v2[:], p2[:], AX)
        r.v("tensor_tensor", sm[:], v1[:], v2[:], ALU.add)
        r.v("reciprocal", sm[:], sm[:])
        r.v("tensor_single_scalar", twt[:], pt[:], v2[:], ALU.is_ge)
        r.v("tensor_tensor", twt[:], twt[:], pt[:], ALU.mult)
        r.v("tensor_scalar_mul", twt[:], twt[:], sm[:])
        r.dma(TW[tc * 128:(tc + 1) * 128, :], twt[:])


def emit_moe(ph):
    r = ph.r
    EIT = ph.IN["EIT"]
    TW = ph.IN["TW"]
    MOE = ph.OUT["MOE"]
    c = emit_const_tiles(ph)

    ewk_bf = ph.dram("ewk_bf", [NE * D, FE], BF16)
    ewv_bf = ph.dram("ewv_bf", [NE * FE, D], BF16)
    ewr_bf = ph.dram("ewr_bf", [NE * D, D], BF16)
    r.dma(ewk_bf, ph.IN["eWk"].rearrange("e d f -> (e d) f"), eng="gpsimd")
    r.dma(ewv_bf, ph.IN["eWv"].rearrange("e f d -> (e f) d"), eng="gpsimd")
    r.dma(ewr_bf, ph.IN["eWr"].rearrange("e d g -> (e d) g"), eng="gpsimd")

    cmk_flat = ph.IN["cm_mu_k"].rearrange("e d -> (e d)")
    cmr_flat = ph.IN["cm_mu_r"].rearrange("e d -> (e d)")
    pscr = ph.ps("pscr", [128, 512], F32)
    muks = [emit_scale_cols(ph, r, c, cmk_flat[e * D:(e + 1) * D], D, f"cmk{e}", pscr)
            for e in range(NE)]
    murs = [emit_scale_cols(ph, r, c, cmr_flat[e * D:(e + 1) * D], D, f"cmr{e}", pscr)
            for e in range(NE)]

    EXXf = ph.dram("EXXf", [D, T], BF16)
    DFMf = ph.dram("DFMf", [D, T], BF16)
    abuf = ph.sb("abuf", [128, S + 16], BF16)
    dtmp = ph.sb("dtmp", [128, S], BF16)
    r.g("memset", abuf[:, 15:16], 0.0)
    for cc in range(8):
        for s in range(NSEQ):
            r.dma(abuf[:, 16:16 + S],
                  EIT[s * S:(s + 1) * S, cc * 128:(cc + 1) * 128], transpose=True)
            r.dma(EXXf[cc * 128:(cc + 1) * 128, s * S:(s + 1) * S],
                  abuf[:, 15:15 + S])
            r.v("tensor_tensor", dtmp[:], abuf[:, 16:16 + S], abuf[:, 15:15 + S],
                ALU.subtract)
            r.dma(DFMf[cc * 128:(cc + 1) * 128, s * S:(s + 1) * S], dtmp[:])

    twsb = ph.sb("twsb", [128, (T // 128) * NE], F32)
    for tc in range(T // 128):
        r.dma(twsb[:, tc * NE:(tc + 1) * NE], TW[tc * 128:(tc + 1) * 128, :])

    xke = [ph.sb(f"xke{k}", [128, S], BF16) for k in range(8)]
    ke = [ph.sb(f"ke{k}", [128, S], BF16) for k in range(FE // 128)]
    exx_t = ph.sb("exx_t", [128, S], BF16)
    dfm_t = ph.sb("dfm_t", [128, S], BF16)
    wkm = ph.sb("wkm", [128, 8 * 128], BF16)
    wvsb = ph.sb("wvsb", [128, (FE // 128) * D], BF16)
    wrsb = ph.sb("wrsb", [128, 8 * D], BF16)
    relu_t = ph.sb("relu_t", [128, 512], F32)
    sig_t = ph.sb("sig_t", [128, 512], F32)
    kvr_t = ph.sb("kvr_t", [128, 512], F32)
    out_t = ph.sb("out_t", [128, 512], F32)
    ps_ke = ph.ps("ps_ke", [128, 512], F32)
    ps_r = ph.ps("ps_r", [128, 512], F32)
    ps_kv = ph.ps("ps_kv", [128, 512], F32)

    for e in range(NE):
        for s in range(NSEQ):
            cols = slice(s * S, (s + 1) * S)
            for k in range(8):
                r.dma(exx_t[:], EXXf[k * 128:(k + 1) * 128, cols])
                r.dma(dfm_t[:], DFMf[k * 128:(k + 1) * 128, cols])
                r.v("scalar_tensor_tensor", xke[k][:], dfm_t[:],
                    muks[e][:, k:k + 1], exx_t[:], ALU.mult, ALU.add)
            for m in range(FE // 128):
                r.dma(wkm[:].rearrange("p (k o) -> p k o", k=8),
                      ewk_bf[e * D:(e + 1) * D, m * 128:(m + 1) * 128]
                      .rearrange("(k p) o -> p k o", p=128))
                for n in range(S // 512):
                    nt = slice(n * 512, (n + 1) * 512)
                    for k in range(8):
                        r.mm(ps_ke[:], wkm[:, k * 128:(k + 1) * 128],
                             xke[k][:, nt], start=(k == 0), stop=(k == 7))
                    r.act(relu_t[:], ps_ke[:], AF.Relu)
                    r.v("tensor_tensor", ke[m][:, nt], relu_t[:], relu_t[:],
                        ALU.mult)
            for k in range(8):
                r.dma(exx_t[:], EXXf[k * 128:(k + 1) * 128, cols])
                r.dma(dfm_t[:], DFMf[k * 128:(k + 1) * 128, cols])
                r.v("scalar_tensor_tensor", xke[k][:], dfm_t[:],
                    murs[e][:, k:k + 1], exx_t[:], ALU.mult, ALU.add)
            for k in range(FE // 128):
                r.dma(wvsb[:, k * D:(k + 1) * D],
                      ewv_bf[e * FE + k * 128:e * FE + (k + 1) * 128, :])
            for k in range(8):
                r.dma(wrsb[:, k * D:(k + 1) * D],
                      ewr_bf[e * D + k * 128:e * D + (k + 1) * 128, :])
            for tc in range(S // 128):
                tcg = s * (S // 128) + tc
                tsl = slice(tc * 128, (tc + 1) * 128)
                for n in range(2):
                    for k in range(8):
                        r.mm(ps_r[:], xke[k][:, tsl],
                             wrsb[:, k * D + n * 512:k * D + (n + 1) * 512],
                             start=(k == 0), stop=(k == 7))
                    r.act(sig_t[:], ps_r[:], AF.Sigmoid)
                    for k in range(FE // 128):
                        r.mm(ps_kv[:], ke[k][:, tsl],
                             wvsb[:, k * D + n * 512:k * D + (n + 1) * 512],
                             start=(k == 0), stop=(k == FE // 128 - 1))
                    r.v("tensor_tensor", kvr_t[:], sig_t[:], ps_kv[:], ALU.mult)
                    r.v("tensor_scalar_mul", out_t[:], kvr_t[:],
                        twsb[:, tcg * NE + e:tcg * NE + e + 1])
                    if e == 0:
                        r.dma(MOE[tcg * 128:(tcg + 1) * 128,
                                  n * 512:(n + 1) * 512], out_t[:], eng="gpsimd")
                    else:
                        r.dma(MOE[tcg * 128:(tcg + 1) * 128,
                                  n * 512:(n + 1) * 512], out_t[:], eng="gpsimd",
                              accum_op=ALU.add)


def emit_ffn1(ph):
    r = ph.r
    XN1 = ph.IN["XN1"]
    ACC = ph.IN["ACC"]
    MOE = ph.IN["MOE"]
    OUT = ph.OUT["OUT"]
    c = emit_const_tiles(ph)

    wsb = ph.sb("wsb", [128, 8 * D], F32)
    for k in range(8):
        r.dma(wsb[:, k * D:(k + 1) * D],
              ph.IN["weight"][k * 128:(k + 1) * 128, :])
    asum = ph.sb("asum", [128, 1], F32)
    at = ph.sb("at", [128, 1], F32)
    r.v("tensor_reduce", asum[:], wsb[:, 0:D], AX, ALU.add, True)
    for k in range(1, 8):
        r.v("tensor_reduce", at[:], wsb[:, k * D:(k + 1) * D], AX, ALU.add, True)
        r.v("tensor_tensor", asum[:], asum[:], at[:], ALU.add)
    ps_s = ph.ps("ps_s", [128, 512], F32)
    r.mm(ps_s[0:1, 0:1], c["ones_c_f"][:], asum[:])
    beta11 = ph.sb("beta11", [1, 1], F32)
    r.v("tensor_scalar_mul", beta11[:], ps_s[0:1, 0:1], 1.0 / (D * D))
    beta128 = ph.sb("beta128", [128, 1], F32)
    r.mm(ps_s[0:128, 0:1], c["ones_r_f"][:], beta11[:])
    r.v("tensor_copy", beta128[:], ps_s[0:128, 0:1])
    brec = ph.sb("brec", [128, 1], F32)
    r.v("tensor_single_scalar", brec[:], beta128[:], 1e-5, ALU.add)
    r.v("reciprocal", brec[:], brec[:])
    wqsb = ph.sb("wqsb", [128, 8 * D], BF16)
    zt = ph.sb("zt", [128, D], F32)
    m1t = ph.sb("m1t", [128, D], F32)
    for k in range(8):
        ksl = slice(k * D, (k + 1) * D)
        r.v("tensor_scalar_mul", zt[:], wsb[:, ksl], brec[:])
        r.v("tensor_single_scalar", m1t[:], zt[:], 0.5, ALU.is_gt)
        r.v("tensor_single_scalar", zt[:], zt[:], -0.5, ALU.is_lt)
        r.v("tensor_tensor", wqsb[:, ksl], m1t[:], zt[:], ALU.subtract)

    gl_row = ph.sb("gl_row", [1, D], F32)
    bl_row = ph.sb("bl_row", [1, D], F32)
    r.dma(gl_row[:], ph.IN["ln_gamma"].rearrange("(o d) -> o d", o=1))
    r.dma(bl_row[:], ph.IN["ln_beta"].rearrange("(o d) -> o d", o=1))
    GLt = ph.sb("GLt", [128, D], F32)
    BLt = ph.sb("BLt", [128, D], F32)
    emit_row_bcast(ph, r, c, gl_row[:], GLt, D, ps_s)
    emit_row_bcast(ph, r, c, bl_row[:], BLt, D, ps_s)
    XNI = ph.dram("XNI", [T, D], BF16)
    xt = ph.sb("xt", [128, D], F32)
    xnit = ph.sb("xnit", [128, D], F32)
    gacc = ph.sb("gacc", [128, 1], F32)
    gtmp = ph.sb("gtmp", [128, 1], F32)
    r.g("memset", gacc[:], 0.0)
    scr = {"sq": ph.sb("sq", [128, D], F32),
           "s1": ph.sb("s1", [128, 1], F32),
           "s2": ph.sb("s2", [128, 1], F32),
           "s3": ph.sb("s3", [128, 1], F32),
           "xn": ph.sb("xnscr", [128, D], F32)}
    xnib = ph.sb("xnib", [128, D], BF16)
    for tc in range(T // 128):
        rows = slice(tc * 128, (tc + 1) * 128)
        r.dma(xt[:], XN1[rows, :])
        emit_ln_tile(ph, r, xt[:], xnit[:], scr, 1e-3, gb_tiles=(GLt, BLt))
        r.v("tensor_copy", xnib[:], xnit[:])
        r.dma(XNI[rows, :], xnib[:])
        r.v("reduce_max", gtmp[:], xnit[:], AX, True)
        r.v("tensor_tensor", gacc[:], gacc[:], gtmp[:], ALU.max)

    g1x1 = ph.sb("g1x1", [1, 1], F32)
    g1x128 = ph.sb("g1x128", [1, 128], F32)
    r.tr(ps_s[0:1, 0:128], gacc[:], c["id_f"][:])
    r.v("reduce_max", g1x1[:], ps_s[0:1, 0:128], AX)
    r.v("tensor_copy", g1x128[:], g1x1[0:1, 0:1].broadcast_to([1, 128]))
    CCIN = ph.dram("CCIN", [1, 128], F32)
    CCOUT = ph.dram("CCOUT", [1, 128], F32)
    r.dma(CCIN, g1x128[:], eng="gpsimd")
    r.comp("gpsimd", "collective_compute", "AllReduce", ALU.max,
           [[0, 1, 2, 3, 4, 5, 6, 7]], [CCIN], [CCOUT])
    gam11 = ph.sb("gam11", [1, 1], F32)
    r.dma(gam11[:], CCOUT[0:1, 0:1])
    gam128 = ph.sb("gam128", [128, 1], F32)
    r.mm(ps_s[0:128, 0:1], c["ones_r_f"][:], gam11[:])
    r.v("tensor_copy", gam128[:], ps_s[0:128, 0:1])
    sxq = ph.sb("sxq", [128, 1], F32)
    r.v("reciprocal", sxq[:], gam128[:])
    r.v("tensor_scalar_mul", sxq[:], sxq[:], 128.0)
    sy = ph.sb("sy", [128, 1], F32)
    r.v("tensor_tensor", sy[:], gam128[:], beta128[:], ALU.mult)
    r.v("tensor_scalar_mul", sy[:], sy[:], 1.0 / 128.0)

    CLIP = 128.0 - 1e-5
    xqf = [ph.sb(f"xqf{k}", [128, T], BF16) for k in range(8)]
    for k in range(8):
        r.dma(xqf[k][:], XNI[0:T, k * 128:(k + 1) * 128], transpose=True)
        r.v("tensor_scalar_mul", xqf[k][:], xqf[k][:], sxq[:])
        r.v("tensor_single_scalar", xqf[k][:], xqf[k][:], CLIP, ALU.min)
        r.v("tensor_single_scalar", xqf[k][:], xqf[k][:], -CLIP, ALU.max)
    psy = ph.ps("psy", [128, 512], F32)
    yt = ph.sb("yt", [128, D], F32)
    acct = ph.sb("acct", [128, D], F32)
    moet = ph.sb("moet", [128, D], F32)
    for tc in range(T // 128):
        rows = slice(tc * 128, (tc + 1) * 128)
        for n in range(2):
            nt = slice(n * 512, (n + 1) * 512)
            for k in range(8):
                r.mm(psy[:], xqf[k][:, tc * 128:(tc + 1) * 128],
                     wqsb[:, k * D + n * 512:k * D + (n + 1) * 512],
                     start=(k == 0), stop=(k == 7))
            r.v("tensor_scalar_mul", yt[:, nt], psy[:], sy[:])
        r.dma(acct[:], ACC[rows, :])
        r.dma(moet[:], MOE[rows, :])
        r.v("tensor_tensor", yt[:], yt[:], acct[:], ALU.add)
        r.v("tensor_tensor", yt[:], yt[:], moet[:], ALU.add)
        r.dma(OUT[rows, :], yt[:])


def kernel_unfused(**inputs):
    x = np.ascontiguousarray(inputs["L0_attn_x"], dtype=np.float32)
    xs = [np.ascontiguousarray(x[2 * i:2 * i + 2].reshape(T, D)) for i in range(NCORES)]
    inp = inputs

    ph = Ph("ln0")
    ph.inp("x", xs)
    ph.out("XN0T", [T, D], BF16)
    emit_ln0(ph)
    XN0T = [o["XN0T"] for o in ph.run()]

    ph = Ph("attn0")
    ph.inp("XN0T", XN0T)
    ph.inp("w_qkv", inp["L0_attn_w_qkv"])
    ph.inp("fc_code", inp["L0_attn_fc_code"])
    ph.inp("w_out", inp["L0_attn_w_out"])
    ph.out("AO", [T, D], F32)
    emit_attn0(ph)
    AO = [o["AO"] for o in ph.run()]

    ph = Ph("ffn0a")
    ph.inp("XN0T", XN0T)
    ph.inp("w1_w", inp["L0_ffn_w1_w"])
    ph.inp("w1_s", inp["L0_ffn_w1_s"])
    ph.inp("w2_w", inp["L0_ffn_w2_w"])
    ph.inp("w2_s", inp["L0_ffn_w2_s"])
    ph.out("H", [DH, T], BF16)
    emit_ffn0a(ph)
    H = [o["H"] for o in ph.run()]

    ph = Ph("ffn0b")
    ph.inp("H", H)
    ph.inp("h1_w", inp["L0_ffn_h1_w"])
    ph.inp("h1_s", inp["L0_ffn_h1_s"])
    ph.out("G1", [DH, T], BF16)
    emit_ffn0bc(ph, "h1_w", "h1_s", "H", "G1")
    G1 = [o["G1"] for o in ph.run()]
    del H

    ph = Ph("ffn0c")
    ph.inp("G1", G1)
    ph.inp("h2_w", inp["L0_ffn_h2_w"])
    ph.inp("h2_s", inp["L0_ffn_h2_s"])
    ph.out("G2", [DH, T], BF16)
    emit_ffn0bc(ph, "h2_w", "h2_s", "G1", "G2")
    G2 = [o["G2"] for o in ph.run()]
    del G1

    ph = Ph("ffn0d")
    ph.inp("G2", G2)
    ph.inp("w3_w", inp["L0_ffn_w3_w"])
    ph.inp("w3_s", inp["L0_ffn_w3_s"])
    ph.inp("x", xs)
    ph.inp("AO", AO)
    ph.out("X1", [T, D], F32)
    emit_ffn0d(ph)
    X1 = [o["X1"] for o in ph.run()]
    del G2, AO

    ph = Ph("l1pre")
    ph.inp("X1", X1)
    for nm in ("ln1_g", "ln1_b", "tm_mu_k", "tm_mu_v", "tm_mu_r",
               "Wk", "Wv", "Wr"):
        ph.inp(nm, inp["L1_attn_" + nm])
    ph.out("XN1", [T, D], F32)
    ph.out("E", [D, T], F32)
    ph.out("V1", [D, T], F32)
    ph.out("R", [D, T], F32)
    emit_l1pre(ph)
    o = ph.run()
    XN1 = [q["XN1"] for q in o]
    E = [q["E"] for q in o]
    V1 = [q["V1"] for q in o]
    R = [q["R"] for q in o]

    ph = Ph("scan")
    ph.inp("E", E)
    ph.inp("V1", V1)
    ph.inp("R", R)
    ph.inp("XN1", XN1)
    ph.inp("X1", X1)
    for nm in ("time_decay", "time_first", "Wo", "ln2_g", "ln2_b"):
        ph.inp(nm, inp["L1_attn_" + nm])
    ph.out("ACC", [T, D], F32)
    ph.out("EIT", [T, D], BF16)
    emit_scan(ph)
    o = ph.run()
    ACC = [q["ACC"] for q in o]
    EIT = [q["EIT"] for q in o]
    del E, V1, R, X1

    ph = Ph("router")
    ph.inp("EIT", EIT)
    ph.inp("router_w", inp["L1_attn_router_w"])
    ph.out("TW", [T, NE], F32)
    emit_router(ph)
    TW = [q["TW"] for q in ph.run()]

    ph = Ph("moe")
    ph.inp("EIT", EIT)
    ph.inp("TW", TW)
    for nm in ("cm_mu_k", "cm_mu_r", "eWk", "eWv", "eWr"):
        ph.inp(nm, inp["L1_attn_" + nm])
    ph.out("MOE", [T, D], F32)
    emit_moe(ph)
    MOE = [q["MOE"] for q in ph.run()]

    ph = Ph("ffn1")
    ph.inp("XN1", XN1)
    ph.inp("ACC", ACC)
    ph.inp("MOE", MOE)
    ph.inp("ln_gamma", inp["L1_ffn_ln_gamma"])
    ph.inp("ln_beta", inp["L1_ffn_ln_beta"])
    ph.inp("weight", inp["L1_ffn_weight"])
    ph.out("OUT", [T, D], F32)
    emit_ffn1(ph)
    OUT = [q["OUT"] for q in ph.run()]

    return np.concatenate([np.asarray(a, np.float32).reshape(2, S, D)
                           for a in OUT], axis=0)



_WMAP = {
    "w_qkv": "L0_attn_w_qkv", "fc_code": "L0_attn_fc_code",
    "w_out": "L0_attn_w_out",
    "w1_w": "L0_ffn_w1_w", "w1_s": "L0_ffn_w1_s",
    "w2_w": "L0_ffn_w2_w", "w2_s": "L0_ffn_w2_s",
    "h1_w": "L0_ffn_h1_w", "h1_s": "L0_ffn_h1_s",
    "h2_w": "L0_ffn_h2_w", "h2_s": "L0_ffn_h2_s",
    "w3_w": "L0_ffn_w3_w", "w3_s": "L0_ffn_w3_s",
    "ln1_g": "L1_attn_ln1_g", "ln1_b": "L1_attn_ln1_b",
    "ln2_g": "L1_attn_ln2_g", "ln2_b": "L1_attn_ln2_b",
    "tm_mu_k": "L1_attn_tm_mu_k", "tm_mu_v": "L1_attn_tm_mu_v",
    "tm_mu_r": "L1_attn_tm_mu_r",
    "time_decay": "L1_attn_time_decay", "time_first": "L1_attn_time_first",
    "Wk": "L1_attn_Wk", "Wv": "L1_attn_Wv", "Wr": "L1_attn_Wr",
    "Wo": "L1_attn_Wo",
    "router_w": "L1_attn_router_w",
    "cm_mu_k": "L1_attn_cm_mu_k", "cm_mu_r": "L1_attn_cm_mu_r",
    "eWk": "L1_attn_eWk", "eWv": "L1_attn_eWv", "eWr": "L1_attn_eWr",
    "weight": "L1_ffn_weight",
    "ln_gamma": "L1_ffn_ln_gamma", "ln_beta": "L1_ffn_ln_beta",
}


def kernel(**inputs):
    x = np.ascontiguousarray(inputs["L0_attn_x"], dtype=np.float32)
    xs = [np.ascontiguousarray(x[2 * i:2 * i + 2].reshape(T, D))
          for i in range(NCORES)]

    ph = Ph("mega")
    W = {"x": ph.inp("x", xs)}
    for short, long in _WMAP.items():
        W[short] = ph.inp(short, np.ascontiguousarray(inputs[long], dtype=np.float32))
    OUTap = ph.out("OUT", [T, D], F32)

    XN0T = ph.dram("XN0Td", [T, D], BF16)
    AOd = ph.dram("AOd", [T, D], F32)
    Hd = ph.dram("Hd", [DH, T], BF16)
    G1d = ph.dram("G1d", [DH, T], BF16)
    G2d = ph.dram("G2d", [DH, T], BF16)
    X1d = ph.dram("X1d", [T, D], F32)
    XN1d = ph.dram("XN1d", [T, D], F32)
    Ed = ph.dram("Ed", [D, T], F32)
    V1d = ph.dram("V1d", [D, T], F32)
    Rd = ph.dram("Rd", [D, T], F32)
    ACCd = ph.dram("ACCd", [T, D], F32)
    EITd = ph.dram("EITd", [T, D], BF16)
    TWd = ph.dram("TWd", [T, NE], F32)
    MOEd = ph.dram("MOEd", [T, D], F32)

    def sec(prefix, emit_fn, IN, OUTm, *args, cons=False):
        ph.begin(prefix)
        ph.IN = IN
        ph.OUT = OUTm
        ph.r.conservative = cons
        emit_fn(ph, *args)
        ph.r.conservative = False
        ph.end()

    sec("p0", emit_ln0, {"x": W["x"]}, {"XN0T": XN0T})
    sec("p1", emit_attn0,
        {"XN0T": XN0T, "w_qkv": W["w_qkv"], "fc_code": W["fc_code"],
         "w_out": W["w_out"]}, {"AO": AOd})
    sec("p2", emit_ffn0a,
        {"XN0T": XN0T, "w1_w": W["w1_w"], "w1_s": W["w1_s"],
         "w2_w": W["w2_w"], "w2_s": W["w2_s"]}, {"H": Hd})
    sec("p3", emit_ffn0bc,
        {"H": Hd, "h1_w": W["h1_w"], "h1_s": W["h1_s"]}, {"G1": G1d},
        "h1_w", "h1_s", "H", "G1")
    sec("p4", emit_ffn0bc,
        {"G1": G1d, "h2_w": W["h2_w"], "h2_s": W["h2_s"]}, {"G2": G2d},
        "h2_w", "h2_s", "G1", "G2")
    sec("p5", emit_ffn0d,
        {"G2": G2d, "w3_w": W["w3_w"], "w3_s": W["w3_s"], "x": W["x"],
         "AO": AOd}, {"X1": X1d})
    sec("p6", emit_l1pre,
        {"X1": X1d, "ln1_g": W["ln1_g"], "ln1_b": W["ln1_b"],
         "tm_mu_k": W["tm_mu_k"], "tm_mu_v": W["tm_mu_v"],
         "tm_mu_r": W["tm_mu_r"], "Wk": W["Wk"], "Wv": W["Wv"],
         "Wr": W["Wr"]},
        {"XN1": XN1d, "E": Ed, "V1": V1d, "R": Rd}, cons=True)
    sec("p7", emit_scan,
        {"E": Ed, "V1": V1d, "R": Rd, "XN1": XN1d, "X1": X1d,
         "time_decay": W["time_decay"], "time_first": W["time_first"],
         "Wo": W["Wo"], "ln2_g": W["ln2_g"], "ln2_b": W["ln2_b"]},
        {"ACC": ACCd, "EIT": EITd}, cons=True)
    sec("p8", emit_router, {"EIT": EITd, "router_w": W["router_w"]},
        {"TW": TWd}, cons=True)
    sec("p9", emit_moe,
        {"EIT": EITd, "TW": TWd, "cm_mu_k": W["cm_mu_k"],
         "cm_mu_r": W["cm_mu_r"], "eWk": W["eWk"], "eWv": W["eWv"],
         "eWr": W["eWr"]}, {"MOE": MOEd})
    sec("p10", emit_ffn1,
        {"XN1": XN1d, "ACC": ACCd, "MOE": MOEd, "ln_gamma": W["ln_gamma"],
         "ln_beta": W["ln_beta"], "weight": W["weight"]}, {"OUT": OUTap}, cons=True)

    OUT = [q["OUT"] for q in ph.run()]
    return np.concatenate([np.asarray(a, np.float32).reshape(2, S, D)
                           for a in OUT], axis=0)
